# version 13
# baseline (speedup 1.0000x reference)
import contextlib

import numpy as np
import ml_dtypes

import concourse.bass as bass
import concourse.mybir as mybir
from concourse.bass_utils import run_bass_kernel_spmd

F32 = mybir.dt.float32
BF16 = mybir.dt.bfloat16
I32 = mybir.dt.int32
AF = mybir.ActivationFunctionType
ALU = mybir.AluOpType
AX = mybir.AxisListType.X

D = 4096
S = 2048
T = 1024
NC = 32
NH = 32
HD = 128
KV = 4
NKT = 16


class _Op:
    __slots__ = ("eng", "fn", "deps", "dma", "need", "cnt", "label", "sem")

    def __init__(self, eng, fn, deps, dma, label):
        self.eng, self.fn, self.deps = eng, fn, deps
        self.dma, self.need, self.cnt, self.label = dma, False, 0, label
        self.sem = "po" if (eng == "gq" and not dma) else eng


class Prog:
    ENGINES = ("pe", "dve", "act", "gq", "sq")
    SEMS = ("pe", "dve", "act", "gq", "sq", "po")

    def __init__(self):
        self.ops = []
        self.streams = {e: [] for e in self.ENGINES}

    def op(self, eng, fn, deps=(), dma=False, label=""):
        o = _Op(eng, fn, list(deps), dma, label)
        self.ops.append(o)
        self.streams[eng].append(o)
        return o

    def emit(self, nc, engines, sems):
        for o in self.ops:
            if o.dma:
                o.need = True
            for d in o.deps:
                d.need = True
        cnt = {s: 0 for s in self.SEMS}
        for e in self.ENGINES:
            for o in self.streams[e]:
                if o.need:
                    cnt[o.sem] += 16 if o.dma else 1
                    o.cnt = cnt[o.sem]
        def emit_stream(eng_name):
            eng = engines[eng_name]
            floors = {}
            for o in self.streams[eng_name]:
                want = {}
                for d in o.deps:
                    if d.eng == eng_name and eng_name == "pe":
                        continue
                    k = d.sem
                    want[k] = max(want.get(k, 0), d.cnt)
                for k, v in want.items():
                    if floors.get(k, 0) < v:
                        eng.wait_ge(sems[k], v)
                        floors[k] = v
                ins = o.fn(eng)
                if o.need:
                    assert ins is not None, f"op {o.label} must return its last instruction"
                    ins.then_inc(sems[o.sem], 16 if o.dma else 1)
        return emit_stream


def _stage(i):
    f32 = np.float32
    x = np.asarray(i["L0_attn_x"], f32)
    Wq = np.asarray(i["L0_attn_Wq_d"], f32)
    bq = np.asarray(i["L0_attn_bq_d"], f32)
    Wqk = np.asarray(i["L0_attn_W_qk"], f32)
    bqk = np.asarray(i["L0_attn_b_qk"], f32)
    Wkv = np.asarray(i["L0_attn_Wkv_d"], f32)
    bkv = np.asarray(i["L0_attn_bkv_d"], f32)
    Wvu = np.asarray(i["L0_attn_Wv_u"], f32)
    bvu = np.asarray(i["L0_attn_bv_u"], f32)
    Wo = np.asarray(i["L0_attn_Wo"], f32)
    bo = np.asarray(i["L0_attn_bo"], f32)
    Wb = np.asarray(i["L0_ffn_W_base"], f32)
    bb = np.asarray(i["L0_ffn_b_base"], f32)
    lA = np.asarray(i["L0_ffn_lora_A"], f32)
    lB = np.asarray(i["L0_ffn_lora_B"], f32)
    rW = np.asarray(i["L0_ffn_router_W"], f32)
    rb = np.asarray(i["L0_ffn_router_b"], f32)
    dp = np.asarray(i["L0_ffn_domain_prior"], f32)
    av = np.asarray(i["L0_ffn_active"], np.int32)

    wqa = np.concatenate([Wkv, np.zeros((D, 1), f32), Wq], axis=1)
    wqs = np.ascontiguousarray(
        wqa.reshape(NC, 128, 17).transpose(1, 0, 2).reshape(128, NC * 17))
    wqab = np.concatenate([bkv, np.zeros(1, f32), bq])[None, :]
    wqk17 = np.concatenate([np.zeros((4, 128), f32), bqk[None, :], Wqk], axis=0)
    wvu5 = np.concatenate([Wvu, bvu[None, :]], axis=0)

    def band_tile(w):
        return np.ascontiguousarray(
            w.reshape(NC, 128, NC, 128).transpose(1, 2, 0, 3).reshape(128, NC * NC * 128))

    wo_t = band_tile(Wo)
    wb_t = band_tile(np.ascontiguousarray(Wb.T))

    la3 = np.ascontiguousarray(lA.reshape(16, 8 * D))
    lb3 = np.ascontiguousarray(lB.transpose(0, 2, 1).reshape(16, 8 * D))

    shared = {
        "wqs": wqs, "wqab": wqab, "wqk17": np.ascontiguousarray(wqk17),
        "wvu": np.ascontiguousarray(wvu5),
        "wo_t": wo_t, "bo2": np.ascontiguousarray(bo.reshape(NC, 128).T), "wb_t": wb_t,
        "bb2": np.ascontiguousarray(bb.reshape(NC, 128).T),
        "la3": la3, "lb3": lb3, "rw": np.ascontiguousarray(rW),
        "rbv": rb[:, None].copy(), "dpv": dp[:, None].copy(),
        "acti": av[None, :].copy(),
    }
    xT = np.ascontiguousarray(x.transpose(0, 2, 1))
    in_maps = []
    for c in range(8):
        b, half = divmod(c, 2)
        if half == 0:
            xt = xT[b]
        else:
            xt = np.concatenate([xT[b][:, T:], xT[b][:, :T]], axis=1)
        m = dict(shared)
        m["xt"] = np.ascontiguousarray(xt)
        in_maps.append(m)
    return in_maps


def _build(detect_races=True):
    nc = bass.Bass("TRN2", target_bir_lowering=False,
                   detect_race_conditions=detect_races)

    xt = nc.dram_tensor("xt", [D, S], F32, kind="ExternalInput")
    wqs_d = nc.dram_tensor("wqs", [128, NC * 17], F32, kind="ExternalInput")
    wqab_d = nc.dram_tensor("wqab", [1, 17], F32, kind="ExternalInput")
    wqk17_d = nc.dram_tensor("wqk17", [17, 128], F32, kind="ExternalInput")
    wvu_d = nc.dram_tensor("wvu", [5, D], F32, kind="ExternalInput")
    wo_d = nc.dram_tensor("wo_t", [128, NC * NC * 128], F32, kind="ExternalInput")
    bo_d = nc.dram_tensor("bo2", [128, NC], F32, kind="ExternalInput")
    wb_d = nc.dram_tensor("wb_t", [128, NC * NC * 128], F32, kind="ExternalInput")
    bb_d = nc.dram_tensor("bb2", [128, NC], F32, kind="ExternalInput")
    la_d = nc.dram_tensor("la3", [16, 8 * D], F32, kind="ExternalInput")
    lb_d = nc.dram_tensor("lb3", [16, 8 * D], F32, kind="ExternalInput")
    rw_d = nc.dram_tensor("rw", [16, D], F32, kind="ExternalInput")
    rbv_d = nc.dram_tensor("rbv", [16, 1], F32, kind="ExternalInput")
    dpv_d = nc.dram_tensor("dpv", [16, 1], F32, kind="ExternalInput")
    act_d = nc.dram_tensor("acti", [1, 4], I32, kind="ExternalInput")
    out_d = nc.dram_tensor("out", [D, T], F32, kind="ExternalOutput")
    xr_d = nc.dram_tensor("xr_scratch", [D, T], F32, kind="Internal")
    x2_d = nc.dram_tensor("x2_scratch", [D, T], F32, kind="Internal")

    bf = ml_dtypes.bfloat16
    iotaB_c = nc.inline_tensor(
        np.tile(np.arange(16, dtype=np.float32)[:, None], (1, 4)), name="iotaB")
    onesf_c = nc.inline_tensor(np.ones((1, 128), np.float32), name="onesf_c")
    ones128f_c = nc.inline_tensor(np.ones((128, 1), np.float32), name="o128f")
    ones128b_c = nc.inline_tensor(np.ones((128, 1), bf), name="o128b")
    onesq_c = nc.inline_tensor(np.ones((128, 128), bf), name="onesq")
    onesbf_c = nc.inline_tensor(np.ones((1, S), bf), name="onesbf")
    mask17_c = nc.inline_tensor(
        np.concatenate([np.zeros((4, 1)), np.ones((13, 1))]).astype(bf), name="mask17")
    maskK_c = nc.inline_tensor(
        np.concatenate([np.ones((4, 1)), np.zeros((13, 1))]).astype(bf), name="maskK")
    ex4 = np.zeros((4, 32), np.float32)
    for m in range(32):
        ex4[m % 4, m] = 1.0
    expand4_c = nc.inline_tensor(ex4, name="expand4")
    twos32_c = nc.inline_tensor(np.full((1, 32), 2.0, np.float32), name="twos32")
    cb = np.zeros((128, 3), np.float32)
    cb[:, 0] = 1e-12
    cb[:, 1] = 1e-5
    cb[:, 2] = float(np.log(2.0))
    cbias_c = nc.inline_tensor(cb, name="cbias")

    ctx = contextlib.ExitStack()
    p = Prog()
    with ctx:
        sb = lambda name, shape, dt: ctx.enter_context(nc.sbuf_tensor(name, shape, dt))
        w1 = sb("w1", [128, D], BF16)
        w2 = sb("w2", [128, D], BF16)
        ubuf = sb("ubuf", [128, NH * T], BF16)
        xb1 = sb("xb1", [128, S], BF16)
        xb2 = sb("xb2", [128, S], BF16)
        e1 = xb1[:, 0:T]
        e2 = xb1[:, T:S]
        vsb = xb2[:, 0:S]
        poolb = sb("poolb", [128, S + D + T], BF16)
        ckvq = poolb[0:17, 0:S]
        wvu_s = poolb[0:5, S:S + D]
        zw_s = poolb[0:32, S + D:S + D + T]
        sq17 = poolb[32:49, 0:S]
        smallb = sb("smallb", [128, 512], BF16)
        oheB0 = smallb[0:16, 0:256]
        oheB32 = smallb[32:48, 0:256]
        ohb0 = smallb[0:16, 256:260]
        ohb32 = smallb[32:48, 256:260]
        ohb64 = smallb[64:80, 256:260]
        maskt = sb("maskt", [128, 2], BF16)
        m17_s = maskt[32:49, 0:1]
        mK_s = maskt[32:49, 1:2]
        wqs_s = sb("wqs_s", [128, NC * 17], BF16)
        wqab_s = sb("wqab_s", [1, 17], BF16)
        wqk_s = sb("wqk_s", [17, 128], BF16)
        bo_s = sb("bo_s", [128, NC], F32)
        bb_s = sb("bb_s", [128, NC], F32)
        zsta = sb("zsta", [128, NC * 36], BF16)
        bt_s = sb("bt_s", [32, D], BF16)
        onesb_s = sb("onesb_s", [1, 512], BF16)
        o128b_s = sb("o128b_s", [128, 1], BF16)
        o128q_s = sb("o128q_s", [128, 128], BF16)
        zck = sb("zck", [128, S], BF16)
        qf1 = sb("qf1", [128, T], BF16)
        qf2 = sb("qf2", [128, T], BF16)
        recdB = sb("recdB", [128, T], F32)
        xres1 = sb("xres1", [128, T], F32)
        xres2 = sb("xres2", [128, T], F32)
        xr1 = sb("xr1", [128, T], F32)
        xr2b = sb("xr2b", [128, T], F32)
        x2f1 = sb("x2f1", [128, T], F32)
        x2f2 = sb("x2f2", [128, T], F32)
        t1f = sb("t1f", [128, T], F32)
        t1g = sb("t1g", [128, T], F32)
        rl1 = sb("rl1", [128, T], F32)
        rl2 = sb("rl2", [128, T], F32)
        aB_s = sb("aB_s", [128, T], F32)
        bB_s = sb("bB_s", [128, T], F32)
        xrb1 = sb("xrb1", [128, T], BF16)
        xrb2 = sb("xrb2", [128, T], BF16)
        sqb1 = sb("sqb1", [128, T], BF16)
        sqb2 = sb("sqb2", [128, T], BF16)
        biasB = sb("biasB", [128, 32], F32)
        onesf_s = sb("onesf_s", [1, 128], F32)
        o128f_s = sb("o128f_s", [128, 1], F32)
        iotaB_s = sb("iotaB_s", [16, 4], F32)
        ex4_s = sb("ex4_s", [4, 32], F32)
        tw32_s = sb("tw32_s", [1, 32], F32)
        acti_s = sb("acti_s", [1, 4], I32)
        misc = sb("misc", [128, 256], F32)
        cb_s = sb("cb_s", [128, 3], F32)
        poolf = sb("poolf", [128, 2 * S], F32)
        rec_h = poolf[0:1, 0:T]
        rec2 = poolf[0:1, 0:T]
        rstd = poolf[0:1, 0:T]
        bsh = poolf[0:1, T:S]
        lexp = poolf[0:4, T:S]
        mns = poolf[0:1, S:S + T]
        vrs = poolf[0:1, S + T:S + 2 * T]
        asb = poolb[32:64, 0:T]
        rsb = poolb[32:64, T:S]
        out1 = xr1
        out2 = xr2b

        actf = misc[0:1, 0:4]
        actb = misc[0:16, 4:8]
        onehot = misc[0:16, 8:12]
        rbp = misc[0:4, 12:14]
        lp = misc[0:4, 14:15]
        abias = misc[0:4, 15:16]
        mwsb = misc[0:1, 16:144]
        mw32 = misc[0:1, 144:176]
        mcq = misc[0:1, 176:177]
        mk2a = misc[0:1, 177:178]
        mk2b = misc[0:1, 178:179]
        mk2 = misc[0:1, 179:180]
        b32 = misc[0:1, 180:212]
        lnr = misc[0:1, 212:213]

        P0 = ctx.enter_context(nc.psum_tensor("P0", [128, T], F32))
        P1 = ctx.enter_context(nc.psum_tensor("P1", [128, T], F32))
        P2 = ctx.enter_context(nc.psum_tensor("P2", [128, T], F32))
        P3 = ctx.enter_context(nc.psum_tensor("P3", [128, T], F32))

        sems = {e: ctx.enter_context(nc.semaphore(f"s_{e}")) for e in Prog.SEMS}
        block = ctx.enter_context(nc.Block())

        def MM(out, lhsT, rhs, start, stop):
            def fn(e, out=out, lhsT=lhsT, rhs=rhs, start=start, stop=stop):
                return e.matmul(out, lhsT, rhs, start=start, stop=stop)
            return fn

        def mm_op(out, lhsT, rhs, deps=(), start=True, stop=True, label=""):
            return p.op("pe", MM(out, lhsT, rhs, start, stop), deps, label=label)

        def mm_pair(out_t, prow, pc0, lhsT, rhs, deps=(), start=True, stop=True, label=""):
            def fn(e, o=out_t, pr=prow, pc0=pc0, l=lhsT, r=rhs, st=start, sp=stop):
                e.matmul(o[0:pr, pc0:pc0 + 512], l, r[:, 0:512], start=st, stop=sp)
                return e.matmul(o[0:pr, pc0 + 512:pc0 + 1024], l, r[:, 512:1024],
                                start=st, stop=sp)
            return p.op("pe", fn, deps, label=label)

        def dma(q, out, in_, deps=(), label=""):
            def fn(e, out=out, in_=in_):
                return e.dma_start(out, in_)
            return p.op(q, fn, deps, dma=True, label=label)

        c_loads = []
        for dst, src in [(onesf_s[:, :], onesf_c[:, :]), (o128f_s[:, :], ones128f_c[:, :]),
                         (o128b_s[:, :], ones128b_c[:, :]),
                         (o128q_s[:, :], onesq_c[:, :]),
                         (onesb_s[:, :], onesbf_c[0:1, 0:512]),
                         (m17_s, mask17_c[:, :]), (mK_s, maskK_c[:, :]),
                         (iotaB_s[:, :], iotaB_c[:, :]), (ex4_s[:, :], expand4_c[:, :]),
                         (tw32_s[:, :], twos32_c[:, :]), (cb_s[:, :], cbias_c[:, :]),
                         (acti_s[:, :], act_d[:, :])]:
            c_loads.append(dma("sq", dst, src, label="const"))
        rb_ld = dma("sq", misc[0:16, 248:249], rbv_d[:, :], label="rbv")
        dp_ld = dma("sq", misc[0:16, 249:250], dpv_d[:, :], label="dpv")
        g_wqs = dma("gq", wqs_s[:, :], wqs_d[:, :], label="wqs")
        g_wqab = dma("gq", wqab_s[:, :], wqab_d[:, :], label="wqab")
        g_wqk = dma("gq", wqk_s[:, :], wqk17_d[:, :], label="wqk")
        g_wvu = dma("gq", wvu_s[:, :], wvu_d[:, :], label="wvu")
        g_bo = dma("sq", bo_s[:, :], bo_d[:, :], label="bo")
        g_bb = dma("sq", bb_s[:, :], bb_d[:, :], label="bb")
        g_rw = dma("gq", w1[64:80, 0:D], rw_d[:, :], label="rw")

        o_zckz = p.op("dve", lambda e: e.memset(zck[:, :], 0.0), [], label="zckz")
        o_qf1z = p.op("dve", lambda e: e.memset(qf1[:, :], 0.0), [o_zckz], label="qf1z")
        o_qf2z = p.op("dve", lambda e: e.memset(qf2[:, :], 0.0), [o_qf1z], label="qf2z")
        o_actf = p.op("dve", lambda e: e.tensor_copy(actf, acti_s[:, :]),
                      [c_loads[-1]], label="actf")
        o_actbc = mm_op(P0[0:16, 0:4], onesf_s[0:1, 0:16], actf, deps=[o_actf, c_loads[0]],
                        label="act_bcast")
        o_actb = p.op("dve", lambda e: e.tensor_copy(actb, P0[0:16, 0:4]), [o_actbc])
        o_onehot = p.op("dve", lambda e: e.tensor_tensor(onehot, actb, iotaB_s[:, :],
                                                         ALU.is_equal),
                        [o_actb, c_loads[6]], label="onehot")
        o_ohb = p.op("dve", lambda e: e.tensor_copy(ohb0, onehot), [o_onehot],
                     label="onehotb")
        o_ohb32 = dma("sq", ohb32, ohb0, [o_ohb], label="ohb32")
        o_ohb64 = dma("sq", ohb64, ohb0, [o_ohb], label="ohb64")
        o_rbs = mm_op(P1[0:4, 0:1], onehot, misc[0:16, 248:249], deps=[o_onehot, rb_ld])
        o_dps = mm_op(P1[0:4, 1:2], onehot, misc[0:16, 249:250], deps=[o_onehot, dp_ld])
        o_rbp = p.op("dve", lambda e: e.tensor_copy(rbp, P1[0:4, 0:2]), [o_rbs, o_dps])
        o_lp = p.op("act", lambda e: e.activation(lp, rbp[:, 1:2], AF.Ln, bias=cb_s[0:4, 0:1]),
                    [o_rbp], label="ln_prior")
        o_abias = p.op("dve", lambda e: e.tensor_add(abias, rbp[:, 0:1], lp), [o_lp, o_rbp])

        la_ld, lb_ld = [], []
        for r in range(8):
            lav_ = w1[0:16, 0:D] if r % 2 == 0 else w1[32:48, 0:D]
            la_ld.append(dma("gq", lav_, la_d[:, r * D:(r + 1) * D], label=f"la{r}"))
        for r in range(8):
            lbv_ = w2[0:16, 0:D] if r % 2 == 0 else w2[32:48, 0:D]
            lb_ld.append(dma("gq", lbv_, lb_d[:, r * D:(r + 1) * D], label=f"lb{r}"))

        zsta_mm, zsta_cp = [], []
        for r in range(8):
            lav_ = w1[0:16, 0:D] if r % 2 == 0 else w1[32:48, 0:D]
            pr = P2[:, 0:128] if r % 2 == 0 else P2[:, 512:640]
            ohv = ohb0 if r % 2 == 0 else ohb32
            deps = [la_ld[r], o_ohb if r % 2 == 0 else o_ohb32]
            if r >= 2:
                deps.append(zsta_cp[r - 2])
            def fn(e, lav_=lav_, pr=pr, ohv=ohv):
                last = None
                for c in range(NC):
                    last = e.matmul(pr[:, c * 4:c * 4 + 4], lav_[:, c * 128:(c + 1) * 128],
                                    ohv, start=True, stop=True)
                return last
            zsta_mm.append(p.op("pe", fn, deps, label=f"zstaMM{r}"))
            zv = zsta.rearrange("p (c n) -> p c n", n=36)[:, :, r * 4:r * 4 + 4]
            prv = pr.rearrange("p (c n) -> p c n", n=4)
            zsta_cp.append(p.op("dve",
                                lambda e, zv=zv, prv=prv: e.tensor_copy(zv, prv),
                                [zsta_mm[r]], label=f"zstaCP{r}"))
        def rw_fn(e):
            last = None
            for c in range(NC):
                last = e.matmul(P3[:, c * 4:c * 4 + 4], w1[64:80, c * 128:(c + 1) * 128],
                                ohb64, start=True, stop=True)
            return last
        o_rwmm = p.op("pe", rw_fn, [g_rw, o_ohb64], label="rwMM")
        zvr = zsta.rearrange("p (c n) -> p c n", n=36)[:, :, 32:36]
        p3v = P3[:, 0:128].rearrange("p (c n) -> p c n", n=4)
        o_rwcp = p.op("dve", lambda e: e.tensor_copy(zvr, p3v), [o_rwmm], label="rwCP")

        o_ohez = p.op("dve", lambda e: e.memset(oheB0, 0.0), [], label="ohe0a")
        o_ohez2 = p.op("dve", lambda e: e.memset(oheB32, 0.0), [o_ohez], label="ohe0b")
        ohe_cp = []
        for r in range(8):
            d0 = r * 32 + r * 4
            tgt = oheB0 if r % 2 == 0 else oheB32
            srcv = ohb0 if r % 2 == 0 else ohb32
            prev = [o_ohb, o_ohez2] if not ohe_cp else [ohe_cp[-1]]
            if r == 1:
                prev = prev + [o_ohb32]
            ohe_cp.append(p.op("dve",
                               lambda e, d0=d0, tgt=tgt, srcv=srcv: e.tensor_copy(
                                   tgt[:, d0:d0 + 4], srcv),
                               prev, label=f"ohe{r}"))
        PT = [P0, P0, P1, P1, P2, P2, P3, P3]
        bt_mm = [None] * 8
        for r in range(8):
            lbv_ = w2[0:16, 0:D] if r % 2 == 0 else w2[32:48, 0:D]
            deps = [lb_ld[r], ohe_cp[-1]]
            if r == 0:
                deps += [o_actbc, o_rbs, o_dps, o_actb, o_rbp,
                         zsta_cp[6], zsta_cp[7], o_rwcp]
            ohv = oheB0 if r % 2 == 0 else oheB32
            def fn(e, r=r, lbv_=lbv_, ohv=ohv):
                last = None
                for t in range(8):
                    last = e.matmul(PT[t][0:32, (t % 2) * 512:(t % 2) * 512 + 512],
                                    ohv[:, r * 32:r * 32 + 32],
                                    lbv_[:, t * 512:(t + 1) * 512],
                                    start=(r == 0), stop=(r == 7))
                return last
            bt_mm[r] = p.op("pe", fn, deps, label=f"btMM{r}")
        bt_cp = []
        for t in range(8):
            bt_cp.append(p.op("dve",
                              lambda e, t=t: e.tensor_copy(
                                  bt_s[:, t * 512:(t + 1) * 512],
                                  PT[t][0:32, (t % 2) * 512:(t % 2) * 512 + 512]),
                              [bt_mm[7]], label=f"btCP{t}"))

        PQ = [P0, P1, P2, P3]
        xb_ld, cq_mm = [], []
        stg = [xres1, xres2, rl1, rl2]
        cast_done = {}
        for c in range(NC):
            buf = xb1 if c % 2 == 0 else xb2
            if c % 2 == 0:
                deps = []
                if c >= 2:
                    deps.append(cq_mm[c - 2])
                xb_ld.append(dma("gq", buf[:, :], xt[c * 128:(c + 1) * 128, :], deps,
                                 label=f"xb{c}"))
                mdeps = [xb_ld[c], g_wqs]
            else:
                sA = stg[2 * ((c // 2) % 2)]
                sB = stg[2 * ((c // 2) % 2) + 1]
                ddeps = [] if c < 4 else [cast_done[c - 4]]
                dA = dma("sq", sA[:, :], xt[c * 128:(c + 1) * 128, 0:T], ddeps,
                         label=f"xq{c}a")
                dB = dma("sq", sB[:, :], xt[c * 128:(c + 1) * 128, T:S], ddeps,
                         label=f"xq{c}b")
                xb_ld.append(dB)
                cdeps = [dA, dB]
                if c >= 2:
                    cdeps.append(cq_mm[c - 2])
                def castfn(e, sA=sA, sB=sB, buf=buf):
                    e.tensor_copy(buf[:, 0:T], sA[:, :])
                    return e.tensor_copy(buf[:, T:S], sB[:, :])
                cast_done[c] = p.op("gq", castfn, cdeps, label=f"xc{c}")
                mdeps = [cast_done[c], g_wqs]
            if c == 0:
                mdeps += bt_cp
            def fn(e, c=c, buf=buf):
                last = None
                for tt in range(4):
                    last = e.matmul(PQ[tt][0:17, 0:512],
                                    wqs_s[:, c * 17:(c + 1) * 17],
                                    buf[:, tt * 512:(tt + 1) * 512],
                                    start=(c == 0), stop=False)
                return last
            cq_mm.append(p.op("pe", fn, mdeps, label=f"cq{c}"))
        def cqb_fn(e):
            last = None
            for tt in range(4):
                last = e.matmul(PQ[tt][0:17, 0:512], wqab_s[:, :], onesb_s[0:1, 0:512],
                                start=False, stop=True)
            return last
        o_cqb = p.op("pe", cqb_fn, [cq_mm[-1], g_wqab, c_loads[3]], label="cq_bias")
        cq_ev = []
        for tt in range(4):
            cq_ev.append(p.op("dve",
                              lambda e, tt=tt: e.tensor_copy(
                                  ckvq[0:17, tt * 512:(tt + 1) * 512], PQ[tt][0:17, 0:512]),
                              [o_cqb], label=f"cqEV{tt}"))
        o_ones4 = dma("sq", ckvq[4:5, 0:S], onesbf_c[:, :], cq_ev, label="ones_row")
        o_zckcp = p.op("dve", lambda e: e.tensor_copy(zck[0:4, 0:S], ckvq[0:4, 0:S]),
                       cq_ev + [o_zckz], label="zckcp")

        o_sq17 = p.op("act", lambda e: e.activation(sq17[:, :], ckvq[:, :], AF.Square),
                      [o_ones4], label="sq17")
        o_mcqm = mm_pair(P0, 1, 0, m17_s[:, :], sq17[:, 0:T], deps=[o_sq17, c_loads[4]],
                         label="mcqMM")
        o_mcq = p.op("dve", lambda e: e.reduce_max(mcq, P0[0:1, 0:T], axis=AX), [o_mcqm])
        o_mka = mm_pair(P1, 1, 0, mK_s[:, :], sq17[:, 0:T], deps=[o_sq17, c_loads[5]])
        o_mkb = mm_pair(P2, 1, 0, mK_s[:, :], sq17[:, T:S], deps=[o_sq17])
        o_mk2a = p.op("dve", lambda e: e.reduce_max(mk2a, P1[0:1, 0:T], axis=AX), [o_mka])
        o_mk2b = p.op("dve", lambda e: e.reduce_max(mk2b, P2[0:1, 0:T], axis=AX), [o_mkb])
        o_mk2 = p.op("dve", lambda e: e.tensor_max(mk2, mk2a, mk2b), [o_mk2a, o_mk2b])
        o_wsq = p.op("act", lambda e: e.activation(sq17[:, 0:128], wqk_s[:, :], AF.Square),
                     [g_wqk, o_mka, o_mkb])
        o_mwm = mm_op(P3[0:1, 0:128], m17_s[:, :], sq17[:, 0:128], deps=[o_wsq, o_mcqm])
        o_mw = p.op("dve", lambda e: e.tensor_copy(mwsb, P3[0:1, 0:128]), [o_mwm])
        mwv = mwsb.rearrange("p (c n) -> p c n", n=4)
        o_mw32 = p.op("dve", lambda e: e.tensor_reduce(mw32.rearrange("p (c n) -> p c n", n=1),
                                                       mwv, AX, ALU.add),
                      [o_mw], label="mw32")
        o_lnw = p.op("act", lambda e: e.activation(b32, mw32, AF.Ln), [o_mw32])
        o_lnc = p.op("act", lambda e: e.activation(mcq, mcq, AF.Ln), [o_mcq, o_mcqm])
        o_lnk = p.op("act", lambda e: e.activation(mk2, mk2, AF.Ln), [o_mk2])
        o_sum3 = p.op("dve", lambda e: e.tensor_scalar(b32, b32, mcq, mk2, ALU.add, ALU.add),
                      [o_lnw, o_lnc, o_lnk], label="sum_ln")
        o_sqB = p.op("act", lambda e: e.activation(b32, b32, AF.Exp, scale=0.5), [o_sum3])
        o_negB = p.op("dve", lambda e: e.tensor_scalar_mul(b32, b32, -0.5), [o_sqB])
        o_bBm = mm_op(P3[0:128, 0:32], onesf_s[:, :], b32, deps=[o_negB, o_mw])
        o_biasB = p.op("dve", lambda e: e.tensor_copy(biasB[:, :], P3[0:128, 0:32]), [o_bBm],
                       label="biasB")

        umul = [None, None]
        prev_qev = None
        prev_exp = {}
        prev_attnv = {}
        prev_head_tail = [o_biasB]
        prev_vev = [o_biasB]
        wo_mm_last = []
        pend_umul = None
        for h in range(32):
            qb = qf1 if h % 2 == 0 else qf2
            vdeps = [o_ones4, g_wvu] + prev_vev
            def vfn(e, h=h):
                last = None
                for kt in range(8):
                    last = e.matmul(P0[:, kt * 128:(kt + 1) * 128],
                                    ckvq[0:5, kt * 128:(kt + 1) * 128],
                                    wvu_s[:, h * 128:(h + 1) * 128], start=True, stop=True)
                return last
            o_v0 = p.op("pe", vfn, vdeps, label=f"v0_{h}")
            def vfn2(e, h=h):
                last = None
                for kt in range(8, 16):
                    last = e.matmul(P1[:, (kt - 8) * 128:(kt - 7) * 128],
                                    ckvq[0:5, kt * 128:(kt + 1) * 128],
                                    wvu_s[:, h * 128:(h + 1) * 128], start=True, stop=True)
                return last
            o_v1 = p.op("pe", vfn2, vdeps, label=f"v1_{h}")
            o_qmm = mm_pair(P3, 4, 0, wqk_s[:, h * 4:h * 4 + 4], ckvq[0:17, 0:T],
                            deps=[o_ones4, g_wqk] + prev_head_tail, label=f"q{h}")
            o_qev = p.op("act",
                         lambda e, qb=qb: e.activation(qb[0:4, 0:T], P3[0:4, 0:T],
                                                       AF.Copy, scale=0.5),
                         [o_qmm, o_qf2z], label=f"qEV{h}")
            o_vev0 = p.op("dve", lambda e: e.tensor_copy(vsb[:, 0:T], P0[:, :]), [o_v0],
                          label=f"vEV0_{h}")
            if pend_umul is not None:
                umul = [p.op("dve", *pend_umul), umul[0]]
                pend_umul = None
            o_vev1 = p.op("act", lambda e: e.activation(vsb[:, T:S], P1[:, :], AF.Copy),
                          [o_v1], label=f"vEV1_{h}")
            prev_exp, prev_attnv = {}, {}
            o_den_last = None
            o_sc_l = {}

            def emit_sc(kt):
                Pp = P0 if kt % 2 == 0 else P1
                sdeps = [o_qev, o_zckcp, o_vev0 if kt % 2 == 0 else o_vev1]
                if kt >= 2:
                    sdeps.append(prev_exp[kt - 2])
                o_sc_l[kt] = mm_pair(Pp, 128, 0, zck[:, kt * 128:(kt + 1) * 128],
                                     qb[:, 0:T], deps=sdeps, label=f"sc{h}_{kt}")

            emit_sc(0)
            emit_sc(1)
            for kt in range(NKT):
                Pp = P0 if kt % 2 == 0 else P1
                ev = e1 if kt % 2 == 0 else e2
                edeps = [o_sc_l[kt], o_biasB]
                if kt >= 2:
                    edeps.append(prev_attnv[kt - 2])
                o_exp = p.op("act",
                             lambda e, Pp=Pp, ev=ev, h=h: e.activation(
                                 ev[:, :], Pp[:, :], AF.Exp, bias=biasB[:, h:h + 1]),
                             edeps, label=f"exp{h}_{kt}")
                prev_exp[kt] = o_exp
                if kt + 2 < NKT:
                    emit_sc(kt + 2)
                ddeps = [o_exp]
                if kt == 0:
                    ddeps.append(o_qev)
                o_den = p.op("pe",
                             lambda e, ev=ev, kt=kt: (
                                 e.matmul(P3[:, 0:512], o128q_s[:, :], ev[:, 0:512],
                                          start=(kt == 0), stop=(kt == NKT - 1)),
                                 e.matmul(P3[:, 512:1024], o128q_s[:, :], ev[:, 512:1024],
                                          start=(kt == 0), stop=(kt == NKT - 1)))[-1],
                             ddeps, label=f"dn{h}_{kt}")
                o_den_last = o_den
                adeps = [o_exp, o_vev0, o_vev1]
                if kt == 0 and umul[0] is not None:
                    adeps.append(umul[0])
                o_av = p.op("pe",
                            lambda e, ev=ev, kt=kt: (
                                e.matmul(P2[:, 0:512], vsb[:, kt * 128:(kt + 1) * 128],
                                         ev[:, 0:512], start=(kt == 0), stop=(kt == NKT - 1)),
                                e.matmul(P2[:, 512:1024], vsb[:, kt * 128:(kt + 1) * 128],
                                         ev[:, 512:1024], start=(kt == 0),
                                         stop=(kt == NKT - 1)))[-1],
                            adeps, label=f"av{h}_{kt}")
                prev_attnv[kt] = o_av
            rdeps = [o_den_last]
            if umul[0] is not None:
                rdeps.append(umul[0])
            o_lnd = p.op("act", lambda e: e.activation(recdB[:, :], P3[:, :], AF.Ln),
                         rdeps, label=f"lnd{h}")
            o_rcd = p.op("act", lambda e: e.activation(recdB[:, :], recdB[:, :], AF.Exp,
                                                       scale=-1.0),
                         [o_lnd], label=f"rcd{h}")
            pend_umul = (
                (lambda e, h=h: e.tensor_tensor(ubuf[:, h * T:(h + 1) * T], P2[:, :],
                                                recdB[:, :], ALU.mult)),
                [prev_attnv[NKT - 1], o_rcd], False, f"umul{h}")
            prev_head_tail = [o_lnd]
            prev_vev = [prev_exp[NKT - 2], prev_exp[NKT - 1]]
        umul = [p.op("dve", *pend_umul), umul[0]]
        o_um = umul[0]
        all_umul = o_um

        wo_ld, wo_mm, xres_ld, xr_add, ln_mm, xr_sp = [], [], [], [], [], []
        act_cp = []

        def emit_ln_mm(oc, mm_list, tag):
            cb = xrb1 if oc % 2 == 0 else xrb2
            sqb = sqb1 if oc % 2 == 0 else sqb2
            sdeps = [act_cp[oc]]
            if oc == 0:
                sdeps += [all_umul] if tag == "ln" else [o_aB, o_bB, o_asb, o_rsb,
                                                         o_alB, o_rcB]
            def sfn(e, oc=oc, cb=cb, sqb=sqb):
                e.matmul(P2[:, 0:512], o128q_s[:, :], cb[:, 0:512],
                         start=(oc == 0), stop=(oc == NC - 1))
                e.matmul(P2[:, 512:1024], o128q_s[:, :], cb[:, 512:1024],
                         start=(oc == 0), stop=(oc == NC - 1))
                e.matmul(P3[:, 0:512], o128q_s[:, :], sqb[:, 0:512],
                         start=(oc == 0), stop=(oc == NC - 1))
                return e.matmul(P3[:, 512:1024], o128q_s[:, :], sqb[:, 512:1024],
                                start=(oc == 0), stop=(oc == NC - 1))
            mm_list.append(p.op("pe", sfn, sdeps, label=f"{tag}MM{oc}"))

        for oc in range(NC):
            wbuf = w1 if oc % 2 == 0 else w2
            if oc == 0:
                ldeps = [o_rwmm]
            elif oc == 1:
                ldeps = [bt_mm[7]]
            else:
                ldeps = [wo_mm[oc - 2]]
            wo_ld.append(dma("gq", wbuf[:, :], wo_d[:, oc * D:(oc + 1) * D], ldeps,
                             label=f"woLD{oc}"))
            Pp = P0 if oc % 2 == 0 else P1
            mdeps = [wo_ld[oc], all_umul, g_bo]
            if oc >= 2:
                mdeps.append(xr_add[oc - 2])
            def fn(e, oc=oc, wbuf=wbuf, Pp=Pp):
                last = None
                for kc in range(NC):
                    e.matmul(Pp[:, 0:512], wbuf[:, kc * 128:(kc + 1) * 128],
                             ubuf[:, kc * T:kc * T + 512],
                             start=(kc == 0), stop=(kc == NC - 1))
                    last = e.matmul(Pp[:, 512:1024], wbuf[:, kc * 128:(kc + 1) * 128],
                                    ubuf[:, kc * T + 512:(kc + 1) * T],
                                    start=(kc == 0), stop=(kc == NC - 1))
                return last
            wo_mm.append(p.op("pe", fn, mdeps, label=f"woMM{oc}"))
            if oc >= 1:
                emit_ln_mm(oc - 1, ln_mm, "ln")
            xrs = xres1 if oc % 2 == 0 else xres2
            xdeps = [cast_done[29]] if oc < 2 else [xr_add[oc - 2]]
            xres_ld.append(dma("sq", xrs[:, :], xt[oc * 128:(oc + 1) * 128, 0:T], xdeps,
                               label=f"xres{oc}"))
            xrb_ = xr1 if oc % 2 == 0 else xr2b
            adeps = [wo_mm[oc], xres_ld[oc]]
            if oc >= 2:
                adeps += [xr_sp[oc - 2], ln_mm[oc - 2] if oc >= 3 else ln_mm[0]]
            xr_add.append(p.op("dve",
                               lambda e, Pp=Pp, xrs=xrs, xrb_=xrb_, oc=oc:
                               e.scalar_tensor_tensor(xrb_[:, :], Pp[:, :],
                                                      bo_s[:, oc:oc + 1], xrs[:, :],
                                                      ALU.add, ALU.add),
                               adeps, label=f"xradd{oc}"))
            cb = xrb1 if oc % 2 == 0 else xrb2
            sqb = sqb1 if oc % 2 == 0 else sqb2
            cdeps = [xr_add[oc]]
            if oc >= 3:
                cdeps.append(ln_mm[oc - 2])
            def cfn(e, xrb_=xrb_, cb=cb, sqb=sqb):
                e.activation(cb[:, :], xrb_[:, :], AF.Copy)
                return e.activation(sqb[:, :], xrb_[:, :], AF.Square)
            act_cp.append(p.op("act", cfn, cdeps, label=f"cast{oc}"))
            xr_sp.append(dma("sq", xr_d[oc * 128:(oc + 1) * 128, :], xrb_[:, :],
                             [xr_add[oc]], label=f"xrSP{oc}"))
        emit_ln_mm(NC - 1, ln_mm, "ln")

        wb_ld_pre = [
            dma("gq", w1[:, :], wb_d[:, 0:D], [wo_mm[NC - 2]], label="wbLD0"),
            dma("gq", w2[:, :], wb_d[:, D:2 * D], [wo_mm[NC - 1]], label="wbLD1"),
        ]

        o_m1 = p.op("dve", lambda e: e.tensor_scalar_mul(mns, P2[0:1, 0:T], 1.0 / D),
                    [ln_mm[-1]], label="mean1")
        o_ms1 = p.op("dve", lambda e: e.tensor_scalar_mul(vrs, P3[0:1, 0:T], 1.0 / D),
                     [ln_mm[-1], o_m1])
        o_m2s = p.op("dve", lambda e: e.tensor_mul(bsh, mns, mns),
                     [o_m1, o_ms1], label="m2")
        o_var1 = p.op("dve", lambda e: e.tensor_sub(vrs, vrs, bsh),
                      [o_m2s], label="var1")
        o_lnv1 = p.op("act", lambda e: e.activation(vrs, vrs, AF.Ln, bias=cb_s[0:1, 1:2]),
                      [o_var1], label="lnv1")
        o_rs1 = p.op("act", lambda e: e.activation(rstd, vrs, AF.Exp, scale=-0.5),
                     [o_lnv1], label="rstd1")
        o_bsh1 = p.op("dve", lambda e: e.scalar_tensor_tensor(bsh, mns, -1.0, rstd, ALU.mult, ALU.mult),
                      [o_rs1, o_var1], label="bsh1")
        o_aBm = p.op("pe",
                     lambda e: (e.matmul(P2[0:128, 0:512], onesf_s[:, :], rstd[0:1, 0:512],
                                         start=True, stop=True),
                                e.matmul(P2[0:128, 512:1024], onesf_s[:, :],
                                         rstd[0:1, 512:1024], start=True, stop=True))[-1],
                     [o_rs1, o_m1, o_ms1], label="aBm")
        o_bBm1 = p.op("pe",
                      lambda e: (e.matmul(P3[0:128, 0:512], onesf_s[:, :], bsh[0:1, 0:512],
                                          start=True, stop=True),
                                 e.matmul(P3[0:128, 512:1024], onesf_s[:, :],
                                          bsh[0:1, 512:1024], start=True, stop=True))[-1],
                      [o_bsh1, o_ms1], label="bBm1")
        o_aB = p.op("act", lambda e: e.activation(aB_s[:, :], P2[:, :], AF.Copy), [o_aBm])
        o_bB = p.op("act", lambda e: e.activation(bB_s[:, :], P3[:, :], AF.Copy), [o_bBm1])

        xr_rl, x2_sp, x2_cp, z_mm = [], [], [], []
        t1_mul, x2_add = [], []
        RL4 = [rl1, rl2, xres1, xres2]
        for c in range(NC):
            rlb = RL4[c % 4]
            rdeps = [xr_sp[c]]
            if c in (0, 1):
                rdeps.append(cast_done[31])
            if c in (2, 3):
                rdeps.append(xr_add[NC - 1])
            xr_rl.append(dma("sq", rlb[:, :], xr_d[c * 128:(c + 1) * 128, :], rdeps,
                             label=f"xrRL{c}"))
        prev_dve, prev_gq = None, None
        for c in range(NC):
            rlb = RL4[c % 4]
            eng = "gq" if c % 3 == 2 else "dve"
            tb = t1f if eng == "dve" else t1g
            xf = x2f1 if eng == "dve" else x2f2
            prev = prev_dve if eng == "dve" else prev_gq
            mdeps = [xr_rl[c], o_aB]
            if prev is not None:
                mdeps.append(prev[1])
            t1_mul.append(p.op(eng,
                               lambda e, rlb=rlb, tb=tb: e.tensor_mul(tb[:, :], rlb[:, :],
                                                                     aB_s[:, :]),
                               mdeps, label=f"t1{c}"))
            adeps = [t1_mul[c], o_bB]
            if prev is not None:
                adeps += [prev[2], prev[3]]
            x2_add.append(p.op(eng,
                               lambda e, xf=xf, tb=tb: e.tensor_add(xf[:, :], tb[:, :],
                                                                    bB_s[:, :]),
                               adeps, label=f"x2add{c}"))
            sp = dma("gq", x2_d[c * 128:(c + 1) * 128, :], xf[:, :], [x2_add[c]],
                     label=f"x2SP{c}")
            x2_sp.append(sp)
            ubdeps = [x2_add[c], wo_mm[NC - 1]]
            cp = p.op("act",
                      lambda e, xf=xf, c=c: e.activation(ubuf[:, c * T:(c + 1) * T],
                                                         xf[:, :], AF.Copy),
                      ubdeps, label=f"x2cp{c}")
            x2_cp.append(cp)
            if eng == "dve":
                prev_dve = (t1_mul[c], x2_add[c], sp, cp)
            else:
                prev_gq = (t1_mul[c], x2_add[c], sp, cp)
            zdeps = [x2_cp[c], bt_cp[0]]
            if c == 0:
                zdeps += [o_aB]
            def zfn(e, c=c):
                e.matmul(P2[0:36, 0:512], zsta[:, c * 36:(c + 1) * 36],
                         ubuf[:, c * T:c * T + 512], start=(c == 0), stop=(c == NC - 1))
                return e.matmul(P2[0:36, 512:1024], zsta[:, c * 36:(c + 1) * 36],
                                ubuf[:, c * T + 512:(c + 1) * T], start=(c == 0),
                                stop=(c == NC - 1))
            z_mm.append(p.op("pe", zfn, zdeps, label=f"z{c}"))
            if c == 0:
                ymm01 = {0: [], 1: []}
            for yo in (0, 1):
                Pp = P0 if yo == 0 else P1
                ydeps = [x2_cp[c], wb_ld_pre[yo]]
                if c == 0:
                    ydeps.append(xr_add[NC - 2 + yo])
                def yfn(e, c=c, yo=yo, Pp=Pp):
                    e.matmul(Pp[:, 0:512], (w1 if yo == 0 else w2)[:, c * 128:(c + 1) * 128],
                             ubuf[:, c * T:c * T + 512], start=(c == 0), stop=False)
                    return e.matmul(Pp[:, 512:1024],
                                    (w1 if yo == 0 else w2)[:, c * 128:(c + 1) * 128],
                                    ubuf[:, c * T + 512:(c + 1) * T], start=(c == 0),
                                    stop=False)
                ymm01[yo].append(p.op("pe", yfn, ydeps, label=f"y{yo}kc{c}"))
        for c in range(4, NC):
            xr_rl[c].deps.append(t1_mul[c - 4])

        o_lex = p.op("act", lambda e: e.activation(lexp, P2[32:36, 0:T], AF.Exp,
                                                   bias=abias),
                     [z_mm[-1], o_abias], label="lexp")
        o_dnm = p.op("pe",
                     lambda e: (e.matmul(P3[0:1, 0:512], o128f_s[0:4, 0:1], lexp[0:4, 0:512],
                                         start=True, stop=True),
                                e.matmul(P3[0:1, 512:1024], o128f_s[0:4, 0:1],
                                         lexp[0:4, 512:1024], start=True, stop=True))[-1],
                     [o_lex, o_bB], label="alpha_dn")
        o_lnd2 = p.op("act", lambda e: e.activation(rec2, P3[0:1, 0:T], AF.Ln),
                      [o_dnm], label="ln_dn2")
        o_rc2 = p.op("act", lambda e: e.activation(rec2, rec2, AF.Exp,
                                                   scale=-1.0, bias=cb_s[0:1, 2:3]),
                     [o_lnd2], label="rec2")
        o_alB = p.op("pe",
                     lambda e: (e.matmul(P3[0:32, 0:512], ex4_s[:, :], lexp[0:4, 0:512],
                                         start=True, stop=True),
                                e.matmul(P3[0:32, 512:1024], ex4_s[:, :], lexp[0:4, 512:1024],
                                         start=True, stop=True))[-1],
                     [o_lex, o_lnd2], label="alphaB")
        o_asb = p.op("act", lambda e: e.activation(asb, P3[0:32, 0:T], AF.Copy),
                     [o_alB], label="asb")
        o_rcB = p.op("pe",
                     lambda e: (e.matmul(P3[0:32, 0:512], onesf_s[0:1, 0:32], rec2[0:1, 0:512],
                                         start=True, stop=True),
                                e.matmul(P3[0:32, 512:1024], onesf_s[0:1, 0:32],
                                         rec2[0:1, 512:1024], start=True, stop=True))[-1],
                     [o_rc2, o_asb], label="recB")
        o_rsb = p.op("act", lambda e: e.activation(rsb, P3[0:32, 0:T], AF.Copy),
                     [o_rcB], label="rsb")
        o_zw1 = p.op("dve", lambda e: e.tensor_mul(asb, asb, rsb),
                     [o_asb, o_rsb], label="zw1")
        o_zw = p.op("dve", lambda e: e.tensor_tensor(zw_s, P2[0:32, 0:T], asb,
                                                     ALU.mult),
                    [o_zw1, z_mm[-1], o_lex], label="zw")

        wb_ld, y_mm, x2_rl, y_add, ln2_mm, y_sp, act2_cp = [], [], [], [], [], [], []

        def emit_ln2_mm(oc):
            cb = xrb1 if oc % 2 == 0 else xrb2
            sqb = sqb1 if oc % 2 == 0 else sqb2
            sdeps = [act2_cp[oc]]
            if oc == 0:
                sdeps += [o_aB, o_bB, o_asb, o_rsb, o_alB, o_rcB]
            def sfn(e, oc=oc, cb=cb, sqb=sqb):
                e.matmul(P2[:, 0:512], o128q_s[:, :], cb[:, 0:512],
                         start=(oc == 0), stop=(oc == NC - 1))
                e.matmul(P2[:, 512:1024], o128q_s[:, :], cb[:, 512:1024],
                         start=(oc == 0), stop=(oc == NC - 1))
                e.matmul(P3[:, 0:512], o128q_s[:, :], sqb[:, 0:512],
                         start=(oc == 0), stop=(oc == NC - 1))
                return e.matmul(P3[:, 512:1024], o128q_s[:, :], sqb[:, 512:1024],
                                start=(oc == 0), stop=(oc == NC - 1))
            ln2_mm.append(p.op("pe", sfn, sdeps, label=f"ln2MM{oc}"))

        for oc in range(NC):
            wbuf = w1 if oc % 2 == 0 else w2
            if oc < 2:
                wb_ld.append(wb_ld_pre[oc])
            else:
                wb_ld.append(dma("gq", wbuf[:, :], wb_d[:, oc * D:(oc + 1) * D],
                                 [y_mm[oc - 2]], label=f"wbLD{oc}"))
            Pp = P0 if oc % 2 == 0 else P1
            mdeps = [wb_ld[oc], x2_cp[NC - 1], o_zw, g_bb] + bt_cp
            if oc >= 2:
                mdeps.append(y_add[oc - 2])
            if oc in (0, 1):
                mdeps.append(ymm01[oc][-1])
            def fn(e, oc=oc, wbuf=wbuf, Pp=Pp):
                if oc >= 2:
                    for kc in range(NC):
                        e.matmul(Pp[:, 0:512], wbuf[:, kc * 128:(kc + 1) * 128],
                                 ubuf[:, kc * T:kc * T + 512], start=(kc == 0), stop=False)
                        e.matmul(Pp[:, 512:1024], wbuf[:, kc * 128:(kc + 1) * 128],
                                 ubuf[:, kc * T + 512:(kc + 1) * T], start=(kc == 0),
                                 stop=False)
                e.matmul(Pp[:, 0:512], bt_s[:, oc * 128:(oc + 1) * 128], zw_s[:, 0:512],
                         start=False, stop=True)
                return e.matmul(Pp[:, 512:1024], bt_s[:, oc * 128:(oc + 1) * 128],
                                zw_s[:, 512:1024], start=False, stop=True)
            y_mm.append(p.op("pe", fn, mdeps, label=f"yMM{oc}"))
            if oc >= 1:
                emit_ln2_mm(oc - 1)
            xrs = xres1 if oc % 2 == 0 else xres2
            rdeps = [x2_sp[oc], t1_mul[NC - 2 + (oc % 2)]]
            if oc >= 2:
                rdeps.append(y_add[oc - 2])
            x2_rl.append(dma("sq", xrs[:, :], x2_d[oc * 128:(oc + 1) * 128, :], rdeps,
                             label=f"x2RL{oc}"))
            xrb_ = xr1 if oc % 2 == 0 else xr2b
            adeps = [y_mm[oc], x2_rl[oc]]
            if oc >= 2:
                adeps += [y_sp[oc - 2], ln2_mm[oc - 2] if oc >= 3 else ln2_mm[0]]
            y_add.append(p.op("dve",
                              lambda e, Pp=Pp, xrs=xrs, xrb_=xrb_, oc=oc:
                              e.scalar_tensor_tensor(xrb_[:, :], Pp[:, :],
                                                     bb_s[:, oc:oc + 1], xrs[:, :],
                                                     ALU.add, ALU.add),
                              adeps, label=f"yadd{oc}"))
            cb = xrb1 if oc % 2 == 0 else xrb2
            sqb = sqb1 if oc % 2 == 0 else sqb2
            cdeps = [y_add[oc]]
            if oc >= 3:
                cdeps.append(ln2_mm[oc - 2])
            def cfn(e, xrb_=xrb_, cb=cb, sqb=sqb):
                e.activation(cb[:, :], xrb_[:, :], AF.Copy)
                return e.activation(sqb[:, :], xrb_[:, :], AF.Square)
            act2_cp.append(p.op("act", cfn, cdeps, label=f"cast2_{oc}"))
            spdeps = [y_add[oc], xr_rl[NC - 1]]
            y_sp.append(dma("sq", xr_d[oc * 128:(oc + 1) * 128, :], xrb_[:, :], spdeps,
                            label=f"ySP{oc}"))
        emit_ln2_mm(NC - 1)

        o_m2 = p.op("dve", lambda e: e.tensor_scalar_mul(mns, P2[0:1, 0:T], 1.0 / D),
                    [ln2_mm[-1], o_bsh1, o_zw], label="mean2")
        o_ms2 = p.op("dve", lambda e: e.tensor_scalar_mul(vrs, P3[0:1, 0:T], 1.0 / D),
                     [ln2_mm[-1], o_m2])
        o_m2b = p.op("dve", lambda e: e.tensor_mul(bsh, mns, mns),
                     [o_m2, o_ms2])
        o_var2 = p.op("dve", lambda e: e.tensor_sub(vrs, vrs, bsh),
                      [o_m2b], label="var2")
        o_lnv2 = p.op("act", lambda e: e.activation(vrs, vrs, AF.Ln, bias=cb_s[0:1, 1:2]),
                      [o_var2], label="lnv2")
        o_rs2 = p.op("act", lambda e: e.activation(rstd, vrs, AF.Exp, scale=-0.5),
                     [o_lnv2], label="rstd2")
        o_bsh2 = p.op("dve", lambda e: e.scalar_tensor_tensor(bsh, mns, -1.0, rstd, ALU.mult, ALU.mult),
                      [o_rs2, o_var2], label="bsh2")
        o_aBm2 = p.op("pe",
                      lambda e: (e.matmul(P2[0:128, 0:512], onesf_s[:, :], rstd[0:1, 0:512],
                                          start=True, stop=True),
                                 e.matmul(P2[0:128, 512:1024], onesf_s[:, :],
                                          rstd[0:1, 512:1024], start=True, stop=True))[-1],
                      [o_rs2, o_m2, o_ms2], label="aBm2")
        o_bBm2 = p.op("pe",
                      lambda e: (e.matmul(P3[0:128, 0:512], onesf_s[:, :], bsh[0:1, 0:512],
                                          start=True, stop=True),
                                 e.matmul(P3[0:128, 512:1024], onesf_s[:, :],
                                          bsh[0:1, 512:1024], start=True, stop=True))[-1],
                      [o_bsh2, o_ms2], label="bBm2")
        o_aB2 = p.op("act", lambda e: e.activation(aB_s[:, :], P2[:, :], AF.Copy),
                     [o_aBm2, t1_mul[NC - 1]], label="aB2")
        o_bB2 = p.op("act", lambda e: e.activation(bB_s[:, :], P3[:, :], AF.Copy),
                     [o_bBm2, x2_add[NC - 1]], label="bB2")

        out_sp = []
        o_t1p, o_o3p = [], []
        y_rl = [None] * NC

        def emit_yrl(c):
            rlb = RL4[c % 4]
            rdeps = [y_sp[c], t1_mul[NC - 1]]
            if c < 4:
                rdeps += [y_add[NC - 1], x2_rl[NC - 1]]
            else:
                rdeps.append(o_t1p[c - 4])
            y_rl[c] = dma("sq", rlb[:, :], xr_d[c * 128:(c + 1) * 128, :], rdeps,
                          label=f"yRL{c}")

        for c in range(4):
            emit_yrl(c)
        prev_dve, prev_gq = None, None
        for c in range(NC):
            rlb = RL4[c % 4]
            eng = "gq" if c % 3 == 2 else "dve"
            tb = t1f if eng == "dve" else t1g
            ob = out1 if eng == "dve" else out2
            prev = prev_dve if eng == "dve" else prev_gq
            mdeps = [y_rl[c], o_aB2, x2_add[NC - 1]]
            if prev is not None:
                mdeps.append(prev[1])
            o_t1 = p.op(eng, lambda e, rlb=rlb, tb=tb: e.tensor_mul(tb[:, :], rlb[:, :],
                                                                   aB_s[:, :]),
                        mdeps, label=f"ot1_{c}")
            o_t1p.append(o_t1)
            odeps = [o_t1, o_bB2, y_sp[NC - 1]]
            if prev is not None:
                odeps.append(prev[2])
            o_o3 = p.op(eng, lambda e, ob=ob, tb=tb: e.tensor_add(ob[:, :], tb[:, :],
                                                                 bB_s[:, :]),
                        odeps, label=f"oadd{c}")
            o_o3p.append(o_o3)
            sp = dma("sq", out_d[c * 128:(c + 1) * 128, :], ob[:, :], [o_o3],
                     label=f"out{c}")
            out_sp.append(sp)
            if c + 4 < NC:
                emit_yrl(c + 4)
            if eng == "dve":
                prev_dve = (o_t1, o_o3, sp)
            else:
                prev_gq = (o_t1, o_o3, sp)
        p.op("sq", lambda e: e.nop(), list(out_sp), label="fin")

        for r in range(2, 8):
            la_ld[r].deps.append(zsta_mm[r - 2])
            lb_ld[r].deps.append(bt_mm[r - 2] if bt_mm[r - 2] is not None else zsta_mm[r - 2])
        for o in p.ops:
            o.deps = [d for d in o.deps if d is not None]

        engines = {"pe": None, "dve": None, "act": None, "gq": None, "sq": None}
        emit = p.emit(nc, engines, sems)

        @block.tensor
        def _(eng):
            engines["pe"] = eng
            emit("pe")

        @block.vector
        def _(eng):
            engines["dve"] = eng
            emit("dve")

        @block.scalar
        def _(eng):
            engines["act"] = eng
            emit("act")

        @block.gpsimd
        def _(eng):
            engines["gq"] = eng
            emit("gq")

        @block.sync
        def _(eng):
            engines["sq"] = eng
            emit("sq")

    return nc


_NC_CACHE = None


def kernel(**inputs):
    global _NC_CACHE
    in_maps = _stage(inputs)
    if _NC_CACHE is None:
        _NC_CACHE = _build()
    res = run_bass_kernel_spmd(_NC_CACHE, in_maps, core_ids=list(range(8)))
    y = np.empty((4, 2048, 4096), np.float32)
    for c in range(8):
        b, half = divmod(c, 2)
        y[b, half * T:(half + 1) * T, :] = res.results[c]["out"].T
    return y


# version 14
# speedup vs baseline: 1.0390x; 1.0390x over previous
import contextlib

import numpy as np
import ml_dtypes

import concourse.bass as bass
import concourse.mybir as mybir
from concourse.bass_utils import run_bass_kernel_spmd

F32 = mybir.dt.float32
BF16 = mybir.dt.bfloat16
I32 = mybir.dt.int32
AF = mybir.ActivationFunctionType
ALU = mybir.AluOpType
AX = mybir.AxisListType.X

D = 4096
S = 2048
T = 1024
NC = 32
NH = 32
HD = 128
KV = 4
NKT = 16


class _Op:
    __slots__ = ("eng", "fn", "deps", "dma", "need", "cnt", "label", "sem")

    def __init__(self, eng, fn, deps, dma, label):
        self.eng, self.fn, self.deps = eng, fn, deps
        self.dma, self.need, self.cnt, self.label = dma, False, 0, label
        self.sem = "po" if (eng == "gq" and not dma) else eng


class Prog:
    ENGINES = ("pe", "dve", "act", "gq", "sq")
    SEMS = ("pe", "dve", "act", "gq", "sq", "po")

    def __init__(self):
        self.ops = []
        self.streams = {e: [] for e in self.ENGINES}

    def op(self, eng, fn, deps=(), dma=False, label=""):
        o = _Op(eng, fn, list(deps), dma, label)
        self.ops.append(o)
        self.streams[eng].append(o)
        return o

    def emit(self, nc, engines, sems):
        for o in self.ops:
            if o.dma:
                o.need = True
            for d in o.deps:
                d.need = True
        cnt = {s: 0 for s in self.SEMS}
        for e in self.ENGINES:
            for o in self.streams[e]:
                if o.need:
                    cnt[o.sem] += 16 if o.dma else 1
                    o.cnt = cnt[o.sem]
        def emit_stream(eng_name):
            eng = engines[eng_name]
            floors = {}
            for o in self.streams[eng_name]:
                want = {}
                for d in o.deps:
                    if d.eng == eng_name and eng_name == "pe":
                        continue
                    k = d.sem
                    want[k] = max(want.get(k, 0), d.cnt)
                for k, v in want.items():
                    if floors.get(k, 0) < v:
                        eng.wait_ge(sems[k], v)
                        floors[k] = v
                ins = o.fn(eng)
                if o.need:
                    assert ins is not None, f"op {o.label} must return its last instruction"
                    ins.then_inc(sems[o.sem], 16 if o.dma else 1)
        return emit_stream


def _stage(i):
    f32 = np.float32
    x = np.asarray(i["L0_attn_x"], f32)
    Wq = np.asarray(i["L0_attn_Wq_d"], f32)
    bq = np.asarray(i["L0_attn_bq_d"], f32)
    Wqk = np.asarray(i["L0_attn_W_qk"], f32)
    bqk = np.asarray(i["L0_attn_b_qk"], f32)
    Wkv = np.asarray(i["L0_attn_Wkv_d"], f32)
    bkv = np.asarray(i["L0_attn_bkv_d"], f32)
    Wvu = np.asarray(i["L0_attn_Wv_u"], f32)
    bvu = np.asarray(i["L0_attn_bv_u"], f32)
    Wo = np.asarray(i["L0_attn_Wo"], f32)
    bo = np.asarray(i["L0_attn_bo"], f32)
    Wb = np.asarray(i["L0_ffn_W_base"], f32)
    bb = np.asarray(i["L0_ffn_b_base"], f32)
    lA = np.asarray(i["L0_ffn_lora_A"], f32)
    lB = np.asarray(i["L0_ffn_lora_B"], f32)
    rW = np.asarray(i["L0_ffn_router_W"], f32)
    rb = np.asarray(i["L0_ffn_router_b"], f32)
    dp = np.asarray(i["L0_ffn_domain_prior"], f32)
    av = np.asarray(i["L0_ffn_active"], np.int32)

    wqa = np.concatenate([Wkv, np.zeros((D, 1), f32), Wq], axis=1)
    wqs = np.ascontiguousarray(
        wqa.reshape(NC, 128, 17).transpose(1, 0, 2).reshape(128, NC * 17))
    wqab = np.concatenate([bkv, np.zeros(1, f32), bq])[None, :]
    wqk17 = np.concatenate([np.zeros((4, 128), f32), bqk[None, :], Wqk], axis=0)
    wvu5 = np.concatenate([Wvu, bvu[None, :]], axis=0)

    def band_tile(w):
        return np.ascontiguousarray(
            w.reshape(NC, 128, NC, 128).transpose(1, 2, 0, 3).reshape(128, NC * NC * 128))

    wo_t = band_tile(Wo)
    wb_t = band_tile(np.ascontiguousarray(Wb.T))

    la3 = np.ascontiguousarray(lA.reshape(16, 8 * D))
    lb3 = np.ascontiguousarray(lB.transpose(0, 2, 1).reshape(16, 8 * D))

    shared = {
        "wqs": wqs, "wqab": wqab, "wqk17": np.ascontiguousarray(wqk17),
        "wvu": np.ascontiguousarray(wvu5),
        "wo_t": wo_t, "bo2": np.ascontiguousarray(bo.reshape(NC, 128).T), "wb_t": wb_t,
        "bb2": np.ascontiguousarray(bb.reshape(NC, 128).T),
        "la3": la3, "lb3": lb3, "rw": np.ascontiguousarray(rW),
        "rbv": rb[:, None].copy(), "dpv": dp[:, None].copy(),
        "acti": av[None, :].copy(),
    }
    xT = np.ascontiguousarray(x.transpose(0, 2, 1))
    in_maps = []
    for c in range(8):
        b, half = divmod(c, 2)
        if half == 0:
            xt = xT[b]
        else:
            xt = np.concatenate([xT[b][:, T:], xT[b][:, :T]], axis=1)
        m = dict(shared)
        m["xt"] = np.ascontiguousarray(xt)
        in_maps.append(m)
    return in_maps


def _build(detect_races=True):
    nc = bass.Bass("TRN2", target_bir_lowering=False,
                   detect_race_conditions=detect_races)

    xt = nc.dram_tensor("xt", [D, S], F32, kind="ExternalInput")
    wqs_d = nc.dram_tensor("wqs", [128, NC * 17], F32, kind="ExternalInput")
    wqab_d = nc.dram_tensor("wqab", [1, 17], F32, kind="ExternalInput")
    wqk17_d = nc.dram_tensor("wqk17", [17, 128], F32, kind="ExternalInput")
    wvu_d = nc.dram_tensor("wvu", [5, D], F32, kind="ExternalInput")
    wo_d = nc.dram_tensor("wo_t", [128, NC * NC * 128], F32, kind="ExternalInput")
    bo_d = nc.dram_tensor("bo2", [128, NC], F32, kind="ExternalInput")
    wb_d = nc.dram_tensor("wb_t", [128, NC * NC * 128], F32, kind="ExternalInput")
    bb_d = nc.dram_tensor("bb2", [128, NC], F32, kind="ExternalInput")
    la_d = nc.dram_tensor("la3", [16, 8 * D], F32, kind="ExternalInput")
    lb_d = nc.dram_tensor("lb3", [16, 8 * D], F32, kind="ExternalInput")
    rw_d = nc.dram_tensor("rw", [16, D], F32, kind="ExternalInput")
    rbv_d = nc.dram_tensor("rbv", [16, 1], F32, kind="ExternalInput")
    dpv_d = nc.dram_tensor("dpv", [16, 1], F32, kind="ExternalInput")
    act_d = nc.dram_tensor("acti", [1, 4], I32, kind="ExternalInput")
    out_d = nc.dram_tensor("out", [D, T], F32, kind="ExternalOutput")
    xr_d = nc.dram_tensor("xr_scratch", [D, T], F32, kind="Internal")
    x2_d = nc.dram_tensor("x2_scratch", [D, T], F32, kind="Internal")

    bf = ml_dtypes.bfloat16
    iotaB_c = nc.inline_tensor(
        np.tile(np.arange(16, dtype=np.float32)[:, None], (1, 4)), name="iotaB")
    onesf_c = nc.inline_tensor(np.ones((1, 128), np.float32), name="onesf_c")
    ones128f_c = nc.inline_tensor(np.ones((128, 1), np.float32), name="o128f")
    ones128b_c = nc.inline_tensor(np.ones((128, 1), bf), name="o128b")
    onesq_c = nc.inline_tensor(np.ones((128, 128), bf), name="onesq")
    onesbf_c = nc.inline_tensor(np.ones((1, S), bf), name="onesbf")
    mask17_c = nc.inline_tensor(
        np.concatenate([np.zeros((4, 1)), np.ones((13, 1))]).astype(bf), name="mask17")
    maskK_c = nc.inline_tensor(
        np.concatenate([np.ones((4, 1)), np.zeros((13, 1))]).astype(bf), name="maskK")
    ex4 = np.zeros((4, 32), np.float32)
    for m in range(32):
        ex4[m % 4, m] = 1.0
    expand4_c = nc.inline_tensor(ex4, name="expand4")
    twos32_c = nc.inline_tensor(np.full((1, 32), 2.0, np.float32), name="twos32")
    cb = np.zeros((128, 3), np.float32)
    cb[:, 0] = 1e-12
    cb[:, 1] = 1e-5
    cb[:, 2] = float(np.log(2.0))
    cbias_c = nc.inline_tensor(cb, name="cbias")

    ctx = contextlib.ExitStack()
    p = Prog()
    with ctx:
        sb = lambda name, shape, dt: ctx.enter_context(nc.sbuf_tensor(name, shape, dt))
        w1 = sb("w1", [128, D], BF16)
        w2 = sb("w2", [128, D], BF16)
        ubuf = sb("ubuf", [128, NH * T], BF16)
        xb1 = sb("xb1", [128, S], BF16)
        xb2 = sb("xb2", [128, S], BF16)
        e1 = xb1[:, 0:T]
        e2 = xb1[:, T:S]
        vsb = xb2[:, 0:S]
        poolb = sb("poolb", [128, S + D + T], BF16)
        ckvq = poolb[0:17, 0:S]
        wvu_s = poolb[0:5, S:S + D]
        zw_s = poolb[0:32, S + D:S + D + T]
        sq17 = poolb[32:49, 0:S]
        smallb = sb("smallb", [128, 512], BF16)
        oheB0 = smallb[0:16, 0:256]
        oheB32 = smallb[32:48, 0:256]
        ohb0 = smallb[0:16, 256:260]
        ohb32 = smallb[32:48, 256:260]
        ohb64 = smallb[64:80, 256:260]
        maskt = sb("maskt", [128, 2], BF16)
        m17_s = maskt[32:49, 0:1]
        mK_s = maskt[32:49, 1:2]
        wqs_s = sb("wqs_s", [128, NC * 17], BF16)
        wqab_s = sb("wqab_s", [1, 17], BF16)
        wqk_s = sb("wqk_s", [17, 128], BF16)
        bo_s = sb("bo_s", [128, NC], F32)
        bb_s = sb("bb_s", [128, NC], F32)
        zsta = sb("zsta", [128, NC * 36], BF16)
        bt_s = sb("bt_s", [32, D], BF16)
        onesb_s = sb("onesb_s", [1, 512], BF16)
        o128b_s = sb("o128b_s", [128, 1], BF16)
        o128q_s = sb("o128q_s", [128, 128], BF16)
        zck = sb("zck", [128, S], BF16)
        qf1 = sb("qf1", [128, T], BF16)
        qf2 = sb("qf2", [128, T], BF16)
        recdB = sb("recdB", [128, T], F32)
        xres1 = sb("xres1", [128, T], F32)
        xres2 = sb("xres2", [128, T], F32)
        xr1 = sb("xr1", [128, T], F32)
        xr2b = sb("xr2b", [128, T], F32)
        x2f1 = sb("x2f1", [128, T], F32)
        x2f2 = sb("x2f2", [128, T], F32)
        t1f = sb("t1f", [128, T], F32)
        t1g = sb("t1g", [128, T], F32)
        rl1 = sb("rl1", [128, T], F32)
        rl2 = sb("rl2", [128, T], F32)
        aB_s = sb("aB_s", [128, T], F32)
        bB_s = sb("bB_s", [128, T], F32)
        xrb1 = sb("xrb1", [128, T], BF16)
        xrb2 = sb("xrb2", [128, T], BF16)
        sqb1 = sb("sqb1", [128, T], BF16)
        sqb2 = sb("sqb2", [128, T], BF16)
        biasB = sb("biasB", [128, 32], F32)
        onesf_s = sb("onesf_s", [1, 128], F32)
        o128f_s = sb("o128f_s", [128, 1], F32)
        iotaB_s = sb("iotaB_s", [16, 4], F32)
        ex4_s = sb("ex4_s", [4, 32], F32)
        tw32_s = sb("tw32_s", [1, 32], F32)
        acti_s = sb("acti_s", [1, 4], I32)
        misc = sb("misc", [128, 256], F32)
        cb_s = sb("cb_s", [128, 3], F32)
        poolf = sb("poolf", [128, 2 * S], F32)
        rec_h = poolf[0:1, 0:T]
        rec2 = poolf[0:1, 0:T]
        rstd = poolf[0:1, 0:T]
        bsh = poolf[0:1, T:S]
        lexp = poolf[0:4, T:S]
        mns = poolf[0:1, S:S + T]
        vrs = poolf[0:1, S + T:S + 2 * T]
        asb = poolb[32:64, 0:T]
        rsb = poolb[32:64, T:S]
        out1 = xr1
        out2 = xr2b

        actf = misc[0:1, 0:4]
        actb = misc[0:16, 4:8]
        onehot = misc[0:16, 8:12]
        rbp = misc[0:4, 12:14]
        lp = misc[0:4, 14:15]
        abias = misc[0:4, 15:16]
        mwsb = misc[0:1, 16:144]
        mw32 = misc[0:1, 144:176]
        mcq = misc[0:1, 176:177]
        mk2a = misc[0:1, 177:178]
        mk2b = misc[0:1, 178:179]
        mk2 = misc[0:1, 179:180]
        b32 = misc[0:1, 180:212]
        lnr = misc[0:1, 212:213]

        P0 = ctx.enter_context(nc.psum_tensor("P0", [128, T], F32))
        P1 = ctx.enter_context(nc.psum_tensor("P1", [128, T], F32))
        P2 = ctx.enter_context(nc.psum_tensor("P2", [128, T], F32))
        P3 = ctx.enter_context(nc.psum_tensor("P3", [128, T], F32))

        sems = {e: ctx.enter_context(nc.semaphore(f"s_{e}")) for e in Prog.SEMS}
        block = ctx.enter_context(nc.Block())

        def MM(out, lhsT, rhs, start, stop):
            def fn(e, out=out, lhsT=lhsT, rhs=rhs, start=start, stop=stop):
                return e.matmul(out, lhsT, rhs, start=start, stop=stop)
            return fn

        def mm_op(out, lhsT, rhs, deps=(), start=True, stop=True, label=""):
            return p.op("pe", MM(out, lhsT, rhs, start, stop), deps, label=label)

        def mm_pair(out_t, prow, pc0, lhsT, rhs, deps=(), start=True, stop=True, label=""):
            def fn(e, o=out_t, pr=prow, pc0=pc0, l=lhsT, r=rhs, st=start, sp=stop):
                e.matmul(o[0:pr, pc0:pc0 + 512], l, r[:, 0:512], start=st, stop=sp)
                return e.matmul(o[0:pr, pc0 + 512:pc0 + 1024], l, r[:, 512:1024],
                                start=st, stop=sp)
            return p.op("pe", fn, deps, label=label)

        def dma(q, out, in_, deps=(), label=""):
            def fn(e, out=out, in_=in_):
                return e.dma_start(out, in_)
            return p.op(q, fn, deps, dma=True, label=label)

        c_loads = []
        for dst, src in [(onesf_s[:, :], onesf_c[:, :]), (o128f_s[:, :], ones128f_c[:, :]),
                         (o128b_s[:, :], ones128b_c[:, :]),
                         (o128q_s[:, :], onesq_c[:, :]),
                         (onesb_s[:, :], onesbf_c[0:1, 0:512]),
                         (m17_s, mask17_c[:, :]), (mK_s, maskK_c[:, :]),
                         (iotaB_s[:, :], iotaB_c[:, :]), (ex4_s[:, :], expand4_c[:, :]),
                         (tw32_s[:, :], twos32_c[:, :]), (cb_s[:, :], cbias_c[:, :]),
                         (acti_s[:, :], act_d[:, :])]:
            c_loads.append(dma("sq", dst, src, label="const"))
        rb_ld = dma("sq", misc[0:16, 248:249], rbv_d[:, :], label="rbv")
        dp_ld = dma("sq", misc[0:16, 249:250], dpv_d[:, :], label="dpv")
        g_wqs = dma("gq", wqs_s[:, :], wqs_d[:, :], label="wqs")
        g_wqab = dma("gq", wqab_s[:, :], wqab_d[:, :], label="wqab")
        g_wqk = dma("gq", wqk_s[:, :], wqk17_d[:, :], label="wqk")
        g_wvu = dma("gq", wvu_s[:, :], wvu_d[:, :], label="wvu")
        g_bo = dma("sq", bo_s[:, :], bo_d[:, :], label="bo")
        g_bb = dma("sq", bb_s[:, :], bb_d[:, :], label="bb")
        g_rw = dma("gq", w1[64:80, 0:D], rw_d[:, :], label="rw")

        o_zckz = p.op("dve", lambda e: e.memset(zck[:, :], 0.0), [], label="zckz")
        o_qf1z = p.op("dve", lambda e: e.memset(qf1[:, :], 0.0), [o_zckz], label="qf1z")
        o_qf2z = p.op("dve", lambda e: e.memset(qf2[:, :], 0.0), [o_qf1z], label="qf2z")
        o_actf = p.op("dve", lambda e: e.tensor_copy(actf, acti_s[:, :]),
                      [c_loads[-1]], label="actf")
        o_actbc = mm_op(P0[0:16, 0:4], onesf_s[0:1, 0:16], actf, deps=[o_actf, c_loads[0]],
                        label="act_bcast")
        o_actb = p.op("dve", lambda e: e.tensor_copy(actb, P0[0:16, 0:4]), [o_actbc])
        o_onehot = p.op("dve", lambda e: e.tensor_tensor(onehot, actb, iotaB_s[:, :],
                                                         ALU.is_equal),
                        [o_actb, c_loads[6]], label="onehot")
        o_ohb = p.op("dve", lambda e: e.tensor_copy(ohb0, onehot), [o_onehot],
                     label="onehotb")
        o_ohb32 = dma("sq", ohb32, ohb0, [o_ohb], label="ohb32")
        o_ohb64 = dma("sq", ohb64, ohb0, [o_ohb], label="ohb64")
        o_rbs = mm_op(P1[0:4, 0:1], onehot, misc[0:16, 248:249], deps=[o_onehot, rb_ld])
        o_dps = mm_op(P1[0:4, 1:2], onehot, misc[0:16, 249:250], deps=[o_onehot, dp_ld])
        o_rbp = p.op("dve", lambda e: e.tensor_copy(rbp, P1[0:4, 0:2]), [o_rbs, o_dps])
        o_lp = p.op("act", lambda e: e.activation(lp, rbp[:, 1:2], AF.Ln, bias=cb_s[0:4, 0:1]),
                    [o_rbp], label="ln_prior")
        o_abias = p.op("dve", lambda e: e.tensor_add(abias, rbp[:, 0:1], lp), [o_lp, o_rbp])

        la_ld, lb_ld = [], []
        for r in range(8):
            lav_ = w1[0:16, 0:D] if r % 2 == 0 else w1[32:48, 0:D]
            la_ld.append(dma("gq", lav_, la_d[:, r * D:(r + 1) * D], label=f"la{r}"))
        for r in range(8):
            lbv_ = w2[0:16, 0:D] if r % 2 == 0 else w2[32:48, 0:D]
            lb_ld.append(dma("gq", lbv_, lb_d[:, r * D:(r + 1) * D], label=f"lb{r}"))

        zsta_mm, zsta_cp = [], []
        for r in range(8):
            lav_ = w1[0:16, 0:D] if r % 2 == 0 else w1[32:48, 0:D]
            pr = P2[:, 0:128] if r % 2 == 0 else P2[:, 512:640]
            ohv = ohb0 if r % 2 == 0 else ohb32
            deps = [la_ld[r], o_ohb if r % 2 == 0 else o_ohb32]
            if r >= 2:
                deps.append(zsta_cp[r - 2])
            def fn(e, lav_=lav_, pr=pr, ohv=ohv):
                last = None
                for c in range(NC):
                    last = e.matmul(pr[:, c * 4:c * 4 + 4], lav_[:, c * 128:(c + 1) * 128],
                                    ohv, start=True, stop=True)
                return last
            zsta_mm.append(p.op("pe", fn, deps, label=f"zstaMM{r}"))
            zv = zsta.rearrange("p (c n) -> p c n", n=36)[:, :, r * 4:r * 4 + 4]
            prv = pr.rearrange("p (c n) -> p c n", n=4)
            zsta_cp.append(p.op("dve",
                                lambda e, zv=zv, prv=prv: e.tensor_copy(zv, prv),
                                [zsta_mm[r]], label=f"zstaCP{r}"))
        def rw_fn(e):
            last = None
            for c in range(NC):
                last = e.matmul(P3[:, c * 4:c * 4 + 4], w1[64:80, c * 128:(c + 1) * 128],
                                ohb64, start=True, stop=True)
            return last
        o_rwmm = p.op("pe", rw_fn, [g_rw, o_ohb64], label="rwMM")
        zvr = zsta.rearrange("p (c n) -> p c n", n=36)[:, :, 32:36]
        p3v = P3[:, 0:128].rearrange("p (c n) -> p c n", n=4)
        o_rwcp = p.op("dve", lambda e: e.tensor_copy(zvr, p3v), [o_rwmm], label="rwCP")

        o_ohez = p.op("dve", lambda e: e.memset(oheB0, 0.0), [], label="ohe0a")
        o_ohez2 = p.op("dve", lambda e: e.memset(oheB32, 0.0), [o_ohez], label="ohe0b")
        ohe_cp = []
        for r in range(8):
            d0 = r * 32 + r * 4
            tgt = oheB0 if r % 2 == 0 else oheB32
            srcv = ohb0 if r % 2 == 0 else ohb32
            prev = [o_ohb, o_ohez2] if not ohe_cp else [ohe_cp[-1]]
            if r == 1:
                prev = prev + [o_ohb32]
            ohe_cp.append(p.op("dve",
                               lambda e, d0=d0, tgt=tgt, srcv=srcv: e.tensor_copy(
                                   tgt[:, d0:d0 + 4], srcv),
                               prev, label=f"ohe{r}"))
        PT = [P0, P0, P1, P1, P2, P2, P3, P3]
        bt_mm = [None] * 8
        for r in range(8):
            lbv_ = w2[0:16, 0:D] if r % 2 == 0 else w2[32:48, 0:D]
            deps = [lb_ld[r], ohe_cp[-1]]
            if r == 0:
                deps += [o_actbc, o_rbs, o_dps, o_actb, o_rbp,
                         zsta_cp[6], zsta_cp[7], o_rwcp]
            ohv = oheB0 if r % 2 == 0 else oheB32
            def fn(e, r=r, lbv_=lbv_, ohv=ohv):
                last = None
                for t in range(8):
                    last = e.matmul(PT[t][0:32, (t % 2) * 512:(t % 2) * 512 + 512],
                                    ohv[:, r * 32:r * 32 + 32],
                                    lbv_[:, t * 512:(t + 1) * 512],
                                    start=(r == 0), stop=(r == 7))
                return last
            bt_mm[r] = p.op("pe", fn, deps, label=f"btMM{r}")
        bt_cp = []
        for t in range(8):
            bt_cp.append(p.op("dve",
                              lambda e, t=t: e.tensor_copy(
                                  bt_s[:, t * 512:(t + 1) * 512],
                                  PT[t][0:32, (t % 2) * 512:(t % 2) * 512 + 512]),
                              [bt_mm[7]], label=f"btCP{t}"))

        PQ = [P0, P1, P2, P3]
        xb_ld, cq_mm = [], []
        stg = [xres1, xres2, rl1, rl2]
        cast_done = {}
        sqc = [c for c in range(NC) if c % 3 != 0]
        for c in range(NC):
            buf = xb1 if c % 2 == 0 else xb2
            if c % 3 == 0:
                deps = []
                if c >= 2:
                    deps.append(cq_mm[c - 2])
                xb_ld.append(dma("gq", buf[:, :], xt[c * 128:(c + 1) * 128, :], deps,
                                 label=f"xb{c}"))
                mdeps = [xb_ld[c], g_wqs]
            else:
                i = sqc.index(c)
                sA = stg[2 * (i % 2)]
                sB = stg[2 * (i % 2) + 1]
                ddeps = [] if i < 2 else [cast_done[sqc[i - 2]]]
                dA = dma("sq", sA[:, :], xt[c * 128:(c + 1) * 128, 0:T], ddeps,
                         label=f"xq{c}a")
                dB = dma("sq", sB[:, :], xt[c * 128:(c + 1) * 128, T:S], ddeps,
                         label=f"xq{c}b")
                xb_ld.append(dB)
                cdeps = [dA, dB]
                if c >= 2:
                    cdeps.append(cq_mm[c - 2])
                def castfn(e, sA=sA, sB=sB, buf=buf):
                    e.tensor_copy(buf[:, 0:T], sA[:, :])
                    return e.tensor_copy(buf[:, T:S], sB[:, :])
                cast_done[c] = p.op("dve", castfn, cdeps, label=f"xc{c}")
                mdeps = [cast_done[c], g_wqs]
            if c == 0:
                mdeps += bt_cp
            def fn(e, c=c, buf=buf):
                last = None
                for tt in range(4):
                    last = e.matmul(PQ[tt][0:17, 0:512],
                                    wqs_s[:, c * 17:(c + 1) * 17],
                                    buf[:, tt * 512:(tt + 1) * 512],
                                    start=(c == 0), stop=False)
                return last
            cq_mm.append(p.op("pe", fn, mdeps, label=f"cq{c}"))
        def cqb_fn(e):
            last = None
            for tt in range(4):
                last = e.matmul(PQ[tt][0:17, 0:512], wqab_s[:, :], onesb_s[0:1, 0:512],
                                start=False, stop=True)
            return last
        o_cqb = p.op("pe", cqb_fn, [cq_mm[-1], g_wqab, c_loads[3]], label="cq_bias")
        cq_ev = []
        for tt in range(4):
            cq_ev.append(p.op("dve",
                              lambda e, tt=tt: e.tensor_copy(
                                  ckvq[0:17, tt * 512:(tt + 1) * 512], PQ[tt][0:17, 0:512]),
                              [o_cqb], label=f"cqEV{tt}"))
        o_ones4 = dma("sq", ckvq[4:5, 0:S], onesbf_c[:, :], cq_ev, label="ones_row")
        o_zckcp = p.op("dve", lambda e: e.tensor_copy(zck[0:4, 0:S], ckvq[0:4, 0:S]),
                       cq_ev + [o_zckz], label="zckcp")

        o_sq17 = p.op("act", lambda e: e.activation(sq17[:, :], ckvq[:, :], AF.Square),
                      [o_ones4], label="sq17")
        o_mcqm = mm_pair(P0, 1, 0, m17_s[:, :], sq17[:, 0:T], deps=[o_sq17, c_loads[4]],
                         label="mcqMM")
        o_mcq = p.op("dve", lambda e: e.reduce_max(mcq, P0[0:1, 0:T], axis=AX), [o_mcqm])
        o_mka = mm_pair(P1, 1, 0, mK_s[:, :], sq17[:, 0:T], deps=[o_sq17, c_loads[5]])
        o_mkb = mm_pair(P2, 1, 0, mK_s[:, :], sq17[:, T:S], deps=[o_sq17])
        o_mk2a = p.op("dve", lambda e: e.reduce_max(mk2a, P1[0:1, 0:T], axis=AX), [o_mka])
        o_mk2b = p.op("dve", lambda e: e.reduce_max(mk2b, P2[0:1, 0:T], axis=AX), [o_mkb])
        o_mk2 = p.op("dve", lambda e: e.tensor_max(mk2, mk2a, mk2b), [o_mk2a, o_mk2b])
        o_wsq = p.op("act", lambda e: e.activation(sq17[:, 0:128], wqk_s[:, :], AF.Square),
                     [g_wqk, o_mka, o_mkb])
        o_mwm = mm_op(P3[0:1, 0:128], m17_s[:, :], sq17[:, 0:128], deps=[o_wsq, o_mcqm])
        o_mw = p.op("dve", lambda e: e.tensor_copy(mwsb, P3[0:1, 0:128]), [o_mwm])
        mwv = mwsb.rearrange("p (c n) -> p c n", n=4)
        o_mw32 = p.op("dve", lambda e: e.tensor_reduce(mw32.rearrange("p (c n) -> p c n", n=1),
                                                       mwv, AX, ALU.add),
                      [o_mw], label="mw32")
        o_lnw = p.op("act", lambda e: e.activation(b32, mw32, AF.Ln), [o_mw32])
        o_lnc = p.op("act", lambda e: e.activation(mcq, mcq, AF.Ln), [o_mcq, o_mcqm])
        o_lnk = p.op("act", lambda e: e.activation(mk2, mk2, AF.Ln), [o_mk2])
        o_sum3 = p.op("dve", lambda e: e.tensor_scalar(b32, b32, mcq, mk2, ALU.add, ALU.add),
                      [o_lnw, o_lnc, o_lnk], label="sum_ln")
        o_sqB = p.op("act", lambda e: e.activation(b32, b32, AF.Exp, scale=0.5), [o_sum3])
        o_negB = p.op("dve", lambda e: e.tensor_scalar_mul(b32, b32, -0.5), [o_sqB])
        o_bBm = mm_op(P3[0:128, 0:32], onesf_s[:, :], b32, deps=[o_negB, o_mw])
        o_biasB = p.op("dve", lambda e: e.tensor_copy(biasB[:, :], P3[0:128, 0:32]), [o_bBm],
                       label="biasB")

        umul = [None, None]
        prev_qev = None
        prev_exp = {}
        prev_attnv = {}
        prev_head_tail = [o_biasB]
        prev_vev = [o_biasB]
        wo_mm_last = []
        pend_umul = None
        for h in range(32):
            qb = qf1 if h % 2 == 0 else qf2
            vdeps = [o_ones4, g_wvu] + prev_vev
            def vfn(e, h=h):
                last = None
                for kt in range(8):
                    last = e.matmul(P0[:, kt * 128:(kt + 1) * 128],
                                    ckvq[0:5, kt * 128:(kt + 1) * 128],
                                    wvu_s[:, h * 128:(h + 1) * 128], start=True, stop=True)
                return last
            o_v0 = p.op("pe", vfn, vdeps, label=f"v0_{h}")
            def vfn2(e, h=h):
                last = None
                for kt in range(8, 16):
                    last = e.matmul(P1[:, (kt - 8) * 128:(kt - 7) * 128],
                                    ckvq[0:5, kt * 128:(kt + 1) * 128],
                                    wvu_s[:, h * 128:(h + 1) * 128], start=True, stop=True)
                return last
            o_v1 = p.op("pe", vfn2, vdeps, label=f"v1_{h}")
            o_qmm = mm_pair(P3, 4, 0, wqk_s[:, h * 4:h * 4 + 4], ckvq[0:17, 0:T],
                            deps=[o_ones4, g_wqk] + prev_head_tail, label=f"q{h}")
            o_qev = p.op("act",
                         lambda e, qb=qb: e.activation(qb[0:4, 0:T], P3[0:4, 0:T],
                                                       AF.Copy, scale=0.5),
                         [o_qmm, o_qf2z], label=f"qEV{h}")
            o_vev0 = p.op("dve", lambda e: e.tensor_copy(vsb[:, 0:T], P0[:, :]), [o_v0],
                          label=f"vEV0_{h}")
            if pend_umul is not None:
                umul = [p.op("dve", *pend_umul), umul[0]]
                pend_umul = None
            o_vev1 = p.op("act", lambda e: e.activation(vsb[:, T:S], P1[:, :], AF.Copy),
                          [o_v1], label=f"vEV1_{h}")
            prev_exp, prev_attnv = {}, {}
            o_den_last = None
            o_sc_l = {}

            def emit_sc(kt):
                Pp = P0 if kt % 2 == 0 else P1
                sdeps = [o_qev, o_zckcp, o_vev0 if kt % 2 == 0 else o_vev1]
                if kt >= 2:
                    sdeps.append(prev_exp[kt - 2])
                o_sc_l[kt] = mm_pair(Pp, 128, 0, zck[:, kt * 128:(kt + 1) * 128],
                                     qb[:, 0:T], deps=sdeps, label=f"sc{h}_{kt}")

            emit_sc(0)
            emit_sc(1)
            for kt in range(NKT):
                Pp = P0 if kt % 2 == 0 else P1
                ev = e1 if kt % 2 == 0 else e2
                edeps = [o_sc_l[kt], o_biasB]
                if kt >= 2:
                    edeps.append(prev_attnv[kt - 2])
                o_exp = p.op("act",
                             lambda e, Pp=Pp, ev=ev, h=h: e.activation(
                                 ev[:, :], Pp[:, :], AF.Exp, bias=biasB[:, h:h + 1]),
                             edeps, label=f"exp{h}_{kt}")
                prev_exp[kt] = o_exp
                if kt + 2 < NKT:
                    emit_sc(kt + 2)
                ddeps = [o_exp]
                if kt == 0:
                    ddeps.append(o_qev)
                o_den = p.op("pe",
                             lambda e, ev=ev, kt=kt: (
                                 e.matmul(P3[:, 0:512], o128q_s[:, :], ev[:, 0:512],
                                          start=(kt == 0), stop=(kt == NKT - 1)),
                                 e.matmul(P3[:, 512:1024], o128q_s[:, :], ev[:, 512:1024],
                                          start=(kt == 0), stop=(kt == NKT - 1)))[-1],
                             ddeps, label=f"dn{h}_{kt}")
                o_den_last = o_den
                adeps = [o_exp, o_vev0, o_vev1]
                if kt == 0 and umul[0] is not None:
                    adeps.append(umul[0])
                o_av = p.op("pe",
                            lambda e, ev=ev, kt=kt: (
                                e.matmul(P2[:, 0:512], vsb[:, kt * 128:(kt + 1) * 128],
                                         ev[:, 0:512], start=(kt == 0), stop=(kt == NKT - 1)),
                                e.matmul(P2[:, 512:1024], vsb[:, kt * 128:(kt + 1) * 128],
                                         ev[:, 512:1024], start=(kt == 0),
                                         stop=(kt == NKT - 1)))[-1],
                            adeps, label=f"av{h}_{kt}")
                prev_attnv[kt] = o_av
            rdeps = [o_den_last]
            if umul[0] is not None:
                rdeps.append(umul[0])
            o_lnd = p.op("act", lambda e: e.activation(recdB[:, :], P3[:, :], AF.Ln),
                         rdeps, label=f"lnd{h}")
            o_rcd = p.op("act", lambda e: e.activation(recdB[:, :], recdB[:, :], AF.Exp,
                                                       scale=-1.0),
                         [o_lnd], label=f"rcd{h}")
            pend_umul = (
                (lambda e, h=h: e.tensor_tensor(ubuf[:, h * T:(h + 1) * T], P2[:, :],
                                                recdB[:, :], ALU.mult)),
                [prev_attnv[NKT - 1], o_rcd], False, f"umul{h}")
            prev_head_tail = [o_lnd]
            prev_vev = [prev_exp[NKT - 2], prev_exp[NKT - 1]]
        umul = [p.op("dve", *pend_umul), umul[0]]
        o_um = umul[0]
        all_umul = o_um

        wo_ld, wo_mm, xres_ld, xr_add, ln_mm, xr_sp = [], [], [], [], [], []
        act_cp = []

        def emit_ln_mm(oc, mm_list, tag):
            cb = xrb1 if oc % 2 == 0 else xrb2
            sqb = sqb1 if oc % 2 == 0 else sqb2
            sdeps = [act_cp[oc]]
            if oc == 0:
                sdeps += [all_umul] if tag == "ln" else [o_aB, o_bB, o_asb, o_rsb,
                                                         o_alB, o_rcB]
            def sfn(e, oc=oc, cb=cb, sqb=sqb):
                e.matmul(P2[:, 0:512], o128q_s[:, :], cb[:, 0:512],
                         start=(oc == 0), stop=(oc == NC - 1))
                e.matmul(P2[:, 512:1024], o128q_s[:, :], cb[:, 512:1024],
                         start=(oc == 0), stop=(oc == NC - 1))
                e.matmul(P3[:, 0:512], o128q_s[:, :], sqb[:, 0:512],
                         start=(oc == 0), stop=(oc == NC - 1))
                return e.matmul(P3[:, 512:1024], o128q_s[:, :], sqb[:, 512:1024],
                                start=(oc == 0), stop=(oc == NC - 1))
            mm_list.append(p.op("pe", sfn, sdeps, label=f"{tag}MM{oc}"))

        for oc in range(NC):
            wbuf = w1 if oc % 2 == 0 else w2
            if oc == 0:
                ldeps = [o_rwmm]
            elif oc == 1:
                ldeps = [bt_mm[7]]
            else:
                ldeps = [wo_mm[oc - 2]]
            wo_ld.append(dma("gq", wbuf[:, :], wo_d[:, oc * D:(oc + 1) * D], ldeps,
                             label=f"woLD{oc}"))
            Pp = P0 if oc % 2 == 0 else P1
            mdeps = [wo_ld[oc], all_umul, g_bo]
            if oc >= 2:
                mdeps.append(xr_add[oc - 2])
            def fn(e, oc=oc, wbuf=wbuf, Pp=Pp):
                last = None
                for kc in range(NC):
                    e.matmul(Pp[:, 0:512], wbuf[:, kc * 128:(kc + 1) * 128],
                             ubuf[:, kc * T:kc * T + 512],
                             start=(kc == 0), stop=(kc == NC - 1))
                    last = e.matmul(Pp[:, 512:1024], wbuf[:, kc * 128:(kc + 1) * 128],
                                    ubuf[:, kc * T + 512:(kc + 1) * T],
                                    start=(kc == 0), stop=(kc == NC - 1))
                return last
            wo_mm.append(p.op("pe", fn, mdeps, label=f"woMM{oc}"))
            if oc >= 1:
                emit_ln_mm(oc - 1, ln_mm, "ln")
            xrs = xres1 if oc % 2 == 0 else xres2
            xdeps = [cast_done[sqc[-2]]] if oc < 2 else [xr_add[oc - 2]]
            xres_ld.append(dma("sq", xrs[:, :], xt[oc * 128:(oc + 1) * 128, 0:T], xdeps,
                               label=f"xres{oc}"))
            xrb_ = xr1 if oc % 2 == 0 else xr2b
            adeps = [wo_mm[oc], xres_ld[oc]]
            if oc >= 2:
                adeps += [xr_sp[oc - 2], ln_mm[oc - 2] if oc >= 3 else ln_mm[0]]
            xr_add.append(p.op("dve",
                               lambda e, Pp=Pp, xrs=xrs, xrb_=xrb_, oc=oc:
                               e.scalar_tensor_tensor(xrb_[:, :], Pp[:, :],
                                                      bo_s[:, oc:oc + 1], xrs[:, :],
                                                      ALU.add, ALU.add),
                               adeps, label=f"xradd{oc}"))
            cb = xrb1 if oc % 2 == 0 else xrb2
            sqb = sqb1 if oc % 2 == 0 else sqb2
            cdeps = [xr_add[oc]]
            if oc >= 3:
                cdeps.append(ln_mm[oc - 2])
            def cfn(e, xrb_=xrb_, cb=cb, sqb=sqb):
                e.activation(cb[:, :], xrb_[:, :], AF.Copy)
                return e.activation(sqb[:, :], xrb_[:, :], AF.Square)
            act_cp.append(p.op("act", cfn, cdeps, label=f"cast{oc}"))
            xr_sp.append(dma("sq", xr_d[oc * 128:(oc + 1) * 128, :], xrb_[:, :],
                             [xr_add[oc]], label=f"xrSP{oc}"))
        emit_ln_mm(NC - 1, ln_mm, "ln")

        wb_ld_pre = [
            dma("gq", w1[:, :], wb_d[:, 0:D], [wo_mm[NC - 2]], label="wbLD0"),
            dma("gq", w2[:, :], wb_d[:, D:2 * D], [wo_mm[NC - 1]], label="wbLD1"),
        ]

        o_m1 = p.op("dve", lambda e: e.tensor_scalar_mul(mns, P2[0:1, 0:T], 1.0 / D),
                    [ln_mm[-1]], label="mean1")
        o_ms1 = p.op("dve", lambda e: e.tensor_scalar_mul(vrs, P3[0:1, 0:T], 1.0 / D),
                     [ln_mm[-1], o_m1])
        o_m2s = p.op("dve", lambda e: e.tensor_mul(bsh, mns, mns),
                     [o_m1, o_ms1], label="m2")
        o_var1 = p.op("dve", lambda e: e.tensor_sub(vrs, vrs, bsh),
                      [o_m2s], label="var1")
        o_lnv1 = p.op("act", lambda e: e.activation(vrs, vrs, AF.Ln, bias=cb_s[0:1, 1:2]),
                      [o_var1], label="lnv1")
        o_rs1 = p.op("act", lambda e: e.activation(rstd, vrs, AF.Exp, scale=-0.5),
                     [o_lnv1], label="rstd1")
        o_bsh1 = p.op("dve", lambda e: e.scalar_tensor_tensor(bsh, mns, -1.0, rstd, ALU.mult, ALU.mult),
                      [o_rs1, o_var1], label="bsh1")
        o_aBm = p.op("pe",
                     lambda e: (e.matmul(P2[0:128, 0:512], onesf_s[:, :], rstd[0:1, 0:512],
                                         start=True, stop=True),
                                e.matmul(P2[0:128, 512:1024], onesf_s[:, :],
                                         rstd[0:1, 512:1024], start=True, stop=True))[-1],
                     [o_rs1, o_m1, o_ms1], label="aBm")
        o_bBm1 = p.op("pe",
                      lambda e: (e.matmul(P3[0:128, 0:512], onesf_s[:, :], bsh[0:1, 0:512],
                                          start=True, stop=True),
                                 e.matmul(P3[0:128, 512:1024], onesf_s[:, :],
                                          bsh[0:1, 512:1024], start=True, stop=True))[-1],
                      [o_bsh1, o_ms1], label="bBm1")
        o_aB = p.op("act", lambda e: e.activation(aB_s[:, :], P2[:, :], AF.Copy), [o_aBm])
        o_bB = p.op("act", lambda e: e.activation(bB_s[:, :], P3[:, :], AF.Copy), [o_bBm1])

        xr_rl, x2_sp, x2_cp, z_mm = [], [], [], []
        t1_mul, x2_add = [], []
        RL4 = [rl1, rl2, xres1, xres2]
        for c in range(NC):
            rlb = RL4[c % 4]
            rdeps = [xr_sp[c]]
            if c in (0, 1):
                rdeps.append(cast_done[sqc[-1]])
            if c in (2, 3):
                rdeps.append(xr_add[NC - 1])
            xr_rl.append(dma("sq", rlb[:, :], xr_d[c * 128:(c + 1) * 128, :], rdeps,
                             label=f"xrRL{c}"))
        prevs = [None, None]
        for c in range(NC):
            rlb = RL4[c % 4]
            eng = "dve"
            tb = t1f if c % 2 == 0 else t1g
            xf = x2f1 if c % 2 == 0 else x2f2
            prev = prevs[c % 2]
            mdeps = [xr_rl[c], o_aB]
            if prev is not None:
                mdeps.append(prev[1])
            t1_mul.append(p.op(eng,
                               lambda e, rlb=rlb, tb=tb: e.tensor_mul(tb[:, :], rlb[:, :],
                                                                     aB_s[:, :]),
                               mdeps, label=f"t1{c}"))
            adeps = [t1_mul[c], o_bB]
            if prev is not None:
                adeps += [prev[2], prev[3]]
            x2_add.append(p.op(eng,
                               lambda e, xf=xf, tb=tb: e.tensor_add(xf[:, :], tb[:, :],
                                                                    bB_s[:, :]),
                               adeps, label=f"x2add{c}"))
            prevs[c % 2] = None
            sp = dma("gq", x2_d[c * 128:(c + 1) * 128, :], xf[:, :], [x2_add[c]],
                     label=f"x2SP{c}")
            x2_sp.append(sp)
            ubdeps = [x2_add[c], wo_mm[NC - 1]]
            cp = p.op("act",
                      lambda e, xf=xf, c=c: e.activation(ubuf[:, c * T:(c + 1) * T],
                                                         xf[:, :], AF.Copy),
                      ubdeps, label=f"x2cp{c}")
            x2_cp.append(cp)
            prevs[c % 2] = (t1_mul[c], x2_add[c], sp, cp)
            zdeps = [x2_cp[c], bt_cp[0]]
            if c == 0:
                zdeps += [o_aB]
            def zfn(e, c=c):
                e.matmul(P2[0:36, 0:512], zsta[:, c * 36:(c + 1) * 36],
                         ubuf[:, c * T:c * T + 512], start=(c == 0), stop=(c == NC - 1))
                return e.matmul(P2[0:36, 512:1024], zsta[:, c * 36:(c + 1) * 36],
                                ubuf[:, c * T + 512:(c + 1) * T], start=(c == 0),
                                stop=(c == NC - 1))
            z_mm.append(p.op("pe", zfn, zdeps, label=f"z{c}"))
            if c == 0:
                ymm01 = {0: [], 1: []}
            for yo in (0, 1):
                Pp = P0 if yo == 0 else P1
                ydeps = [x2_cp[c], wb_ld_pre[yo]]
                if c == 0:
                    ydeps.append(xr_add[NC - 2 + yo])
                def yfn(e, c=c, yo=yo, Pp=Pp):
                    e.matmul(Pp[:, 0:512], (w1 if yo == 0 else w2)[:, c * 128:(c + 1) * 128],
                             ubuf[:, c * T:c * T + 512], start=(c == 0), stop=False)
                    return e.matmul(Pp[:, 512:1024],
                                    (w1 if yo == 0 else w2)[:, c * 128:(c + 1) * 128],
                                    ubuf[:, c * T + 512:(c + 1) * T], start=(c == 0),
                                    stop=False)
                ymm01[yo].append(p.op("pe", yfn, ydeps, label=f"y{yo}kc{c}"))
        for c in range(4, NC):
            xr_rl[c].deps.append(t1_mul[c - 4])

        o_lex = p.op("act", lambda e: e.activation(lexp, P2[32:36, 0:T], AF.Exp,
                                                   bias=abias),
                     [z_mm[-1], o_abias], label="lexp")
        o_dnm = p.op("pe",
                     lambda e: (e.matmul(P3[0:1, 0:512], o128f_s[0:4, 0:1], lexp[0:4, 0:512],
                                         start=True, stop=True),
                                e.matmul(P3[0:1, 512:1024], o128f_s[0:4, 0:1],
                                         lexp[0:4, 512:1024], start=True, stop=True))[-1],
                     [o_lex, o_bB], label="alpha_dn")
        o_lnd2 = p.op("act", lambda e: e.activation(rec2, P3[0:1, 0:T], AF.Ln),
                      [o_dnm], label="ln_dn2")
        o_rc2 = p.op("act", lambda e: e.activation(rec2, rec2, AF.Exp,
                                                   scale=-1.0, bias=cb_s[0:1, 2:3]),
                     [o_lnd2], label="rec2")
        o_alB = p.op("pe",
                     lambda e: (e.matmul(P3[0:32, 0:512], ex4_s[:, :], lexp[0:4, 0:512],
                                         start=True, stop=True),
                                e.matmul(P3[0:32, 512:1024], ex4_s[:, :], lexp[0:4, 512:1024],
                                         start=True, stop=True))[-1],
                     [o_lex, o_lnd2], label="alphaB")
        o_asb = p.op("act", lambda e: e.activation(asb, P3[0:32, 0:T], AF.Copy),
                     [o_alB], label="asb")
        o_rcB = p.op("pe",
                     lambda e: (e.matmul(P3[0:32, 0:512], onesf_s[0:1, 0:32], rec2[0:1, 0:512],
                                         start=True, stop=True),
                                e.matmul(P3[0:32, 512:1024], onesf_s[0:1, 0:32],
                                         rec2[0:1, 512:1024], start=True, stop=True))[-1],
                     [o_rc2, o_asb], label="recB")
        o_rsb = p.op("act", lambda e: e.activation(rsb, P3[0:32, 0:T], AF.Copy),
                     [o_rcB], label="rsb")
        o_zw1 = p.op("dve", lambda e: e.tensor_mul(asb, asb, rsb),
                     [o_asb, o_rsb], label="zw1")
        o_zw = p.op("dve", lambda e: e.tensor_tensor(zw_s, P2[0:32, 0:T], asb,
                                                     ALU.mult),
                    [o_zw1, z_mm[-1], o_lex], label="zw")

        wb_ld, y_mm, x2_rl, y_add, ln2_mm, y_sp, act2_cp = [], [], [], [], [], [], []

        def emit_ln2_mm(oc):
            cb = xrb1 if oc % 2 == 0 else xrb2
            sqb = sqb1 if oc % 2 == 0 else sqb2
            sdeps = [act2_cp[oc]]
            if oc == 0:
                sdeps += [o_aB, o_bB, o_asb, o_rsb, o_alB, o_rcB]
            def sfn(e, oc=oc, cb=cb, sqb=sqb):
                e.matmul(P2[:, 0:512], o128q_s[:, :], cb[:, 0:512],
                         start=(oc == 0), stop=(oc == NC - 1))
                e.matmul(P2[:, 512:1024], o128q_s[:, :], cb[:, 512:1024],
                         start=(oc == 0), stop=(oc == NC - 1))
                e.matmul(P3[:, 0:512], o128q_s[:, :], sqb[:, 0:512],
                         start=(oc == 0), stop=(oc == NC - 1))
                return e.matmul(P3[:, 512:1024], o128q_s[:, :], sqb[:, 512:1024],
                                start=(oc == 0), stop=(oc == NC - 1))
            ln2_mm.append(p.op("pe", sfn, sdeps, label=f"ln2MM{oc}"))

        for oc in range(NC):
            wbuf = w1 if oc % 2 == 0 else w2
            if oc < 2:
                wb_ld.append(wb_ld_pre[oc])
            else:
                wb_ld.append(dma("gq", wbuf[:, :], wb_d[:, oc * D:(oc + 1) * D],
                                 [y_mm[oc - 2]], label=f"wbLD{oc}"))
            Pp = P0 if oc % 2 == 0 else P1
            mdeps = [wb_ld[oc], x2_cp[NC - 1], o_zw, g_bb] + bt_cp
            if oc >= 2:
                mdeps.append(y_add[oc - 2])
            if oc in (0, 1):
                mdeps.append(ymm01[oc][-1])
            def fn(e, oc=oc, wbuf=wbuf, Pp=Pp):
                if oc >= 2:
                    for kc in range(NC):
                        e.matmul(Pp[:, 0:512], wbuf[:, kc * 128:(kc + 1) * 128],
                                 ubuf[:, kc * T:kc * T + 512], start=(kc == 0), stop=False)
                        e.matmul(Pp[:, 512:1024], wbuf[:, kc * 128:(kc + 1) * 128],
                                 ubuf[:, kc * T + 512:(kc + 1) * T], start=(kc == 0),
                                 stop=False)
                e.matmul(Pp[:, 0:512], bt_s[:, oc * 128:(oc + 1) * 128], zw_s[:, 0:512],
                         start=False, stop=True)
                return e.matmul(Pp[:, 512:1024], bt_s[:, oc * 128:(oc + 1) * 128],
                                zw_s[:, 512:1024], start=False, stop=True)
            y_mm.append(p.op("pe", fn, mdeps, label=f"yMM{oc}"))
            if oc >= 1:
                emit_ln2_mm(oc - 1)
            xrs = xres1 if oc % 2 == 0 else xres2
            rdeps = [x2_sp[oc], t1_mul[NC - 2 + (oc % 2)]]
            if oc >= 2:
                rdeps.append(y_add[oc - 2])
            x2_rl.append(dma("sq", xrs[:, :], x2_d[oc * 128:(oc + 1) * 128, :], rdeps,
                             label=f"x2RL{oc}"))
            xrb_ = xr1 if oc % 2 == 0 else xr2b
            adeps = [y_mm[oc], x2_rl[oc]]
            if oc >= 2:
                adeps += [y_sp[oc - 2], ln2_mm[oc - 2] if oc >= 3 else ln2_mm[0]]
            y_add.append(p.op("dve",
                              lambda e, Pp=Pp, xrs=xrs, xrb_=xrb_, oc=oc:
                              e.scalar_tensor_tensor(xrb_[:, :], Pp[:, :],
                                                     bb_s[:, oc:oc + 1], xrs[:, :],
                                                     ALU.add, ALU.add),
                              adeps, label=f"yadd{oc}"))
            cb = xrb1 if oc % 2 == 0 else xrb2
            sqb = sqb1 if oc % 2 == 0 else sqb2
            cdeps = [y_add[oc]]
            if oc >= 3:
                cdeps.append(ln2_mm[oc - 2])
            def cfn(e, xrb_=xrb_, cb=cb, sqb=sqb):
                e.activation(cb[:, :], xrb_[:, :], AF.Copy)
                return e.activation(sqb[:, :], xrb_[:, :], AF.Square)
            act2_cp.append(p.op("act", cfn, cdeps, label=f"cast2_{oc}"))
            spdeps = [y_add[oc], xr_rl[NC - 1]]
            y_sp.append(dma("sq", xr_d[oc * 128:(oc + 1) * 128, :], xrb_[:, :], spdeps,
                            label=f"ySP{oc}"))
        emit_ln2_mm(NC - 1)

        o_m2 = p.op("dve", lambda e: e.tensor_scalar_mul(mns, P2[0:1, 0:T], 1.0 / D),
                    [ln2_mm[-1], o_bsh1, o_zw], label="mean2")
        o_ms2 = p.op("dve", lambda e: e.tensor_scalar_mul(vrs, P3[0:1, 0:T], 1.0 / D),
                     [ln2_mm[-1], o_m2])
        o_m2b = p.op("dve", lambda e: e.tensor_mul(bsh, mns, mns),
                     [o_m2, o_ms2])
        o_var2 = p.op("dve", lambda e: e.tensor_sub(vrs, vrs, bsh),
                      [o_m2b], label="var2")
        o_lnv2 = p.op("act", lambda e: e.activation(vrs, vrs, AF.Ln, bias=cb_s[0:1, 1:2]),
                      [o_var2], label="lnv2")
        o_rs2 = p.op("act", lambda e: e.activation(rstd, vrs, AF.Exp, scale=-0.5),
                     [o_lnv2], label="rstd2")
        o_bsh2 = p.op("dve", lambda e: e.scalar_tensor_tensor(bsh, mns, -1.0, rstd, ALU.mult, ALU.mult),
                      [o_rs2, o_var2], label="bsh2")
        o_aBm2 = p.op("pe",
                      lambda e: (e.matmul(P2[0:128, 0:512], onesf_s[:, :], rstd[0:1, 0:512],
                                          start=True, stop=True),
                                 e.matmul(P2[0:128, 512:1024], onesf_s[:, :],
                                          rstd[0:1, 512:1024], start=True, stop=True))[-1],
                      [o_rs2, o_m2, o_ms2], label="aBm2")
        o_bBm2 = p.op("pe",
                      lambda e: (e.matmul(P3[0:128, 0:512], onesf_s[:, :], bsh[0:1, 0:512],
                                          start=True, stop=True),
                                 e.matmul(P3[0:128, 512:1024], onesf_s[:, :],
                                          bsh[0:1, 512:1024], start=True, stop=True))[-1],
                      [o_bsh2, o_ms2], label="bBm2")
        o_aB2 = p.op("act", lambda e: e.activation(aB_s[:, :], P2[:, :], AF.Copy),
                     [o_aBm2, t1_mul[NC - 1]], label="aB2")
        o_bB2 = p.op("act", lambda e: e.activation(bB_s[:, :], P3[:, :], AF.Copy),
                     [o_bBm2, x2_add[NC - 1]], label="bB2")

        out_sp = []
        o_t1p, o_o3p = [], []
        y_rl = [None] * NC
        OUT4 = [xr1, xr2b, x2f1, x2f2]

        def emit_yrl(c):
            rlb = RL4[c % 4]
            if c < 2:
                rdeps = [y_sp[c], t1_mul[NC - 1]]
            elif c < 4:
                rdeps = [y_sp[c], y_add[NC - 2 + (c % 2)]]
            else:
                rdeps = [y_sp[c], o_t1p[c - 4]]
            y_rl[c] = dma("sq", rlb[:, :], xr_d[c * 128:(c + 1) * 128, :], rdeps,
                          label=f"yRL{c}")

        for c in range(4):
            emit_yrl(c)
        for c in range(NC):
            rlb = RL4[c % 4]
            tb = t1f if c % 2 == 0 else t1g
            ob = OUT4[c % 4]
            mdeps = [y_rl[c], o_aB2, x2_add[NC - 1]]
            if c >= 2:
                mdeps.append(o_o3p[c - 2])
            o_t1 = p.op("dve", lambda e, rlb=rlb, tb=tb: e.tensor_mul(tb[:, :], rlb[:, :],
                                                                     aB_s[:, :]),
                        mdeps, label=f"ot1_{c}")
            o_t1p.append(o_t1)
            odeps = [o_t1, o_bB2, y_sp[NC - 1]]
            if c >= 4:
                odeps.append(out_sp[c - 4])
            o_o3 = p.op("dve", lambda e, ob=ob, tb=tb: e.tensor_add(ob[:, :], tb[:, :],
                                                                   bB_s[:, :]),
                        odeps, label=f"oadd{c}")
            o_o3p.append(o_o3)
            sp = dma("gq", out_d[c * 128:(c + 1) * 128, :], ob[:, :], [o_o3],
                     label=f"out{c}")
            out_sp.append(sp)
            if c + 4 < NC:
                emit_yrl(c + 4)
        p.op("sq", lambda e: e.nop(), list(out_sp), label="fin")

        for r in range(2, 8):
            la_ld[r].deps.append(zsta_mm[r - 2])
            lb_ld[r].deps.append(bt_mm[r - 2] if bt_mm[r - 2] is not None else zsta_mm[r - 2])
        for o in p.ops:
            o.deps = [d for d in o.deps if d is not None]

        engines = {"pe": None, "dve": None, "act": None, "gq": None, "sq": None}
        emit = p.emit(nc, engines, sems)

        @block.tensor
        def _(eng):
            engines["pe"] = eng
            emit("pe")

        @block.vector
        def _(eng):
            engines["dve"] = eng
            emit("dve")

        @block.scalar
        def _(eng):
            engines["act"] = eng
            emit("act")

        @block.gpsimd
        def _(eng):
            engines["gq"] = eng
            emit("gq")

        @block.sync
        def _(eng):
            engines["sq"] = eng
            emit("sq")

    return nc


_NC_CACHE = None


def kernel(**inputs):
    global _NC_CACHE
    in_maps = _stage(inputs)
    if _NC_CACHE is None:
        _NC_CACHE = _build()
    res = run_bass_kernel_spmd(_NC_CACHE, in_maps, core_ids=list(range(8)))
    y = np.empty((4, 2048, 4096), np.float32)
    for c in range(8):
        b, half = divmod(c, 2)
        y[b, half * T:(half + 1) * T, :] = res.results[c]["out"].T
    return y
